# version 10
# speedup vs baseline: 1.5785x; 1.0676x over previous
import numpy as np
import ml_dtypes
from contextlib import ExitStack

import concourse.bass as bass
import concourse.mybir as mybir
from concourse.bass_utils import run_bass_kernel_spmd

F32 = mybir.dt.float32
BF16 = mybir.dt.bfloat16
AF = mybir.ActivationFunctionType
ALU = mybir.AluOpType
AX = mybir.AxisListType

D = 2048
T = 2048
NT = 16
CH = 16
H = 16
HID = 5632
NHT = 44
E = 6
R = 16
SCALE = float(1.0 / np.sqrt(128.0))
LN_EPS = 1e-5
Q_EPS = 1e-8

EXT = [j + 1 for j in range(NT)]


def _chunks(w):
    out = []
    off = 0
    while off < w:
        cw = min(512, w - off)
        out.append((off, cw))
        off += cw
    return out


class _DPair:

    def __init__(self, sems):
        self.sems = sems
        self.w = len(sems)


def dw(eng, s, n):
    if isinstance(s, _DPair):
        for r in range(s.w):
            cnt = (n - r + s.w - 1) // s.w
            if cnt > 0:
                eng.wait_ge(s.sems[r], 16 * cnt)
    else:
        eng.wait_ge(s, 16 * n)


class _SerialQ:

    def __init__(self, eng):
        self._e = eng
        self._counts = {}

    def dma_start(self, out, in_, **kw):
        ins = self._e.dma_start(out=out, in_=in_, **kw)
        return _PendInc(self, ins)

    def __getattr__(self, k):
        return getattr(self._e, k)


class _PendInc:
    def __init__(self, q, ins):
        self._q = q
        self._ins = ins

    def then_inc(self, sem, k):
        q = self._q
        if isinstance(sem, _DPair):
            i = q._counts.get(id(sem), 0)
            self._ins.then_inc(sem.sems[i % sem.w], 16)
            q._counts[id(sem)] = i + 1
            nxt = i + 1
            if nxt >= sem.w:
                q._e.wait_ge(sem.sems[nxt % sem.w], 16 * (nxt // sem.w))
            return self._ins
        self._ins.then_inc(sem, k)
        key = id(sem)
        q._counts[key] = q._counts.get(key, 0) + k
        q._e.wait_ge(sem, q._counts[key])
        return self._ins


_SER_OPS = {
    "tensor_tensor", "tensor_add", "tensor_sub", "tensor_mul", "tensor_max",
    "tensor_scalar", "tensor_single_scalar", "tensor_scalar_add", "tensor_scalar_sub",
    "tensor_scalar_mul", "tensor_scalar_max", "tensor_scalar_min", "tensor_relu",
    "tensor_reduce", "reduce_max", "reduce_sum", "scalar_tensor_tensor",
    "tensor_tensor_scan", "tensor_copy", "memset", "memzero", "reciprocal",
    "select", "activation", "sign", "square", "sqrt", "mul", "copy",
    "add", "bn_stats", "bn_aggr", "pool", "pool_avg", "pool_max",
}


class _SerialEng:

    def __init__(self, eng, sem):
        self._e = eng
        self._s = sem
        self.n = 0

    def __getattr__(self, k):
        a = getattr(self._e, k)
        if k not in _SER_OPS:
            return a

        def wrap(*args, **kw):
            if self.n > 0:
                self._e.wait_ge(self._s, self.n)
            ins = a(*args, **kw)
            ins.then_inc(self._s, 1)
            self.n += 1
            return ins
        return wrap




class FCtx:

    def __init__(self, nc, es, block):
        self.nc = nc
        self.es = es
        self.block = block
        self.vss = es.enter_context(nc.semaphore("vss"))
        self.sss = es.enter_context(nc.semaphore("sss"))
        self.vw = None
        self.sw = None
        self.gq = _SerialQ(None)
        self.syq = _SerialQ(None)
        self._nsem = 0
        self._keep = []
        self.bar = {"events": [], "vss": 0, "sss": 0}

    def sem(self, name):
        self._nsem += 1
        return self.es.enter_context(self.nc.semaphore(f"{name}{self._nsem}"))

    def dsem(self, name, w=2):
        p = _DPair([self.sem(name + chr(97 + i)) for i in range(w)])
        self._keep.append(p)
        return p

    def sbuf(self, name, shape, dt, **kw):
        self._nsem += 1
        return self.nc.sbuf_tensor(f"{name}_{self._nsem}", shape, dt, **kw)

    def psum(self, name, shape, dt=F32, **kw):
        self._nsem += 1
        return self.nc.psum_tensor(f"{name}_{self._nsem}", shape, dt, **kw)

    def V(self, raw):
        if self.vw is None:
            self.vw = _SerialEng(raw, self.vss)
        else:
            self.vw._e = raw
        return self.vw

    def S(self, raw):
        if self.sw is None:
            self.sw = _SerialEng(raw, self.sss)
        else:
            self.sw._e = raw
        return self.sw

    def G(self, raw):
        self.gq._e = raw
        return self.gq

    def SY(self, raw):
        self.syq._e = raw
        return self.syq

    def sw_n(self):
        return self.sw.n if self.sw is not None else 0

    def vw_n(self):
        return self.vw.n if self.vw is not None else 0

    def wait_bar(self, eng):
        for (sem, val) in self.bar["events"]:
            if isinstance(sem, _DPair):
                dw(eng, sem, val)
            else:
                eng.wait_ge(sem, val)
        if self.bar["vss"] > 0:
            eng.wait_ge(self.vss, self.bar["vss"])
        if self.bar["sss"] > 0:
            eng.wait_ge(self.sss, self.bar["sss"])

    def end_phase(self, events):
        self.bar = {"events": list(events), "vss": self.vw_n(), "sss": self.sw_n()}

def _nc():
    return bass.Bass("TRN2", target_bir_lowering=False, debug=False)


def emit_ln(cx, xT, xnT):
    nc = cx.nc
    es = ExitStack()
    with es:
        x_sb = es.enter_context(cx.sbuf("x_sb", [128, CH * T], F32))
        xn_sb = es.enter_context(cx.sbuf("xn_sb", [128, CH * T], BF16))
        x2_sb = es.enter_context(cx.sbuf("x2_sb", [128, 512], F32))
        ones_c = es.enter_context(cx.sbuf("ones_c", [128, 1], F32))
        ones_r = es.enter_context(cx.sbuf("ones_r", [1, 128], F32))
        epsv = es.enter_context(cx.sbuf("epsv", [1, 1], F32))
        mean_sb = es.enter_context(cx.sbuf("mean_sb", [1, 512], F32))
        var_sb = es.enter_context(cx.sbuf("var_sb", [1, 512], F32))
        rstd_sb = es.enter_context(cx.sbuf("rstd_sb", [1, 512], F32))
        tmp_sb = es.enter_context(cx.sbuf("tmp_sb", [128, 512], F32))
        ps_s = es.enter_context(cx.psum("ps_s", [1, 512], F32))
        ps_s2 = es.enter_context(cx.psum("ps_s2", [1, 512], F32))
        ps_bm = es.enter_context(cx.psum("ps_bm", [128, 512], F32))
        ps_br = es.enter_context(cx.psum("ps_br", [128, 512], F32))
        ld = cx.dsem("ld")
        mmd = cx.sem("mmd")
        pass
        pass
        st = cx.dsem("st")
        block = cx.block
        vss = cx.vss
        sss = cx.sss
        SS0 = cx.sw_n()
        ev = {}

        @block.vector
        def _(v):
            v = cx.V(v)
            cx.wait_bar(v)
            v.memset(ones_c[:, :], 1.0)
            v.memset(epsv[0:1, :], LN_EPS)
            v.memset(ones_r[:, :], 1.0)
            ev["ones"] = v.n
            for q4 in range(4):
                if q4 > 0:
                    v.wait_ge(sss, SS0 + 2 * q4)
                for c in range(CH):
                    dw(v, ld, (c + 1))
                    if q4 * CH + c > 0:
                        v.wait_ge(mmd, q4 * 17 + c)
                    src = x_sb[:, c * T + q4 * 512: c * T + (q4 + 1) * 512]
                    v.tensor_mul(x2_sb[:, :], src, src)
                    ev[f"x2_{q4}_{c}"] = v.n
                v.wait_ge(mmd, q4 * 17 + 16)
                v.tensor_scalar_mul(mean_sb[0:1, :], ps_s[0:1, :], 1.0 / D)
                v.tensor_scalar_mul(var_sb[0:1, :], ps_s2[0:1, :], 1.0 / D)
                v.tensor_mul(tmp_sb[0:1, :], mean_sb[0:1, :], mean_sb[0:1, :])
                v.tensor_sub(var_sb[0:1, :], var_sb[0:1, :], tmp_sb[0:1, :])
                ev[f"stats_{q4}"] = v.n
                v.wait_ge(mmd, 17 * (q4 + 1))
                for c in range(CH):
                    src = x_sb[:, c * T + q4 * 512: c * T + (q4 + 1) * 512]
                    v.tensor_sub(tmp_sb[:, :], src, ps_bm[:, :])
                    v.tensor_mul(xn_sb[:, c * T + q4 * 512: c * T + (q4 + 1) * 512],
                                 tmp_sb[:, :], ps_br[:, :])
                ev[f"xn_{q4}"] = v.n
            ev["xn_all"] = v.n

        @block.scalar
        def _(s):
            s = cx.S(s)
            cx.wait_bar(s)
            for q4 in range(4):
                s.wait_ge(vss, ev[f"stats_{q4}"])
                s.activation(rstd_sb[0:1, :], var_sb[0:1, :], AF.Ln, bias=epsv[0:1, 0:1])
                s.activation(rstd_sb[0:1, :], rstd_sb[0:1, :], AF.Exp, scale=-0.5)

        @block.tensor
        def _(t):
            cx.wait_bar(t)
            t.wait_ge(vss, ev["ones"])
            for q4 in range(4):
                if q4 > 0:
                    t.wait_ge(vss, ev[f"stats_{q4 - 1}"])
                    t.wait_ge(sss, SS0 + 2 * q4)
                for c in range(CH):
                    dw(t, ld, (c + 1))
                    t.wait_ge(vss, ev[f"x2_{q4}_{c}"])
                    t.matmul(ps_s[0:1, :], ones_c[:, 0:1],
                             x_sb[:, c * T + q4 * 512: c * T + (q4 + 1) * 512],
                             start=(c == 0), stop=(c == CH - 1))
                    t.matmul(ps_s2[0:1, :], ones_c[:, 0:1], x2_sb[:, :],
                             start=(c == 0), stop=(c == CH - 1)).then_inc(mmd, 1)
                t.wait_ge(sss, SS0 + 2 * (q4 + 1))
                t.wait_ge(vss, ev[f"stats_{q4}"])
                if q4 > 0:
                    t.wait_ge(vss, ev[f"xn_{q4 - 1}"])
                t.matmul(ps_bm[:, :], ones_r[0:1, :], mean_sb[0:1, :], start=True, stop=True)
                t.matmul(ps_br[:, :], ones_r[0:1, :], rstd_sb[0:1, :],
                         start=True, stop=True).then_inc(mmd, 1)

        @block.gpsimd
        def _(g):
            g = cx.G(g)
            cx.wait_bar(g)
            for c in range(CH):
                g.dma_start(out=x_sb[:, c * T:(c + 1) * T],
                            in_=xT[c * 128:(c + 1) * 128, :]).then_inc(ld, 16)
            g.wait_ge(vss, ev["xn_all"])
            for c in range(CH):
                g.dma_start(out=xnT[c * 128:(c + 1) * 128, :],
                            in_=xn_sb[:, c * T:(c + 1) * T]).then_inc(st, 16)
    cx.end_phase([(ld, 16), (mmd, 68), (st, 16)])



def emit_qkv(cx, xnT, w, q_tok, k_sw, v_tok):
    jobs = [(0, q_tok, "tok"), (D, k_sw, "feat"), (2 * D, v_tok, "tok")]
    jobs = [(0, q_tok, "tok"), (D, k_sw, "feat"), (2 * D, v_tok, "tok")]
    nc = cx.nc
    es = ExitStack()
    with es:
        xn_sb = es.enter_context(cx.sbuf("xn_sb", [128, CH * T], BF16))
        w_sb = es.enter_context(cx.sbuf("w_sb", [128, CH * D], BF16))
        stage = es.enter_context(cx.sbuf("stage", [128, 2 * D], BF16))
        ps = [es.enter_context(cx.psum(f"ps{i}", [128, D], F32)) for i in range(2)]
        ld = cx.dsem("ld")
        mmg = cx.sem("mmg")
        pass
        st = cx.dsem("st")
        block = cx.block
        vss = cx.vss
        sss = cx.sss
        SS0 = cx.sw_n()

        @block.scalar
        def _(s):
            s = cx.S(s)
            cx.wait_bar(s)
            for n in range(48):
                s.wait_ge(mmg, n + 1)
                if n >= 2:
                    dw(s, st, (n - 1))
                s.activation(stage[:, (n % 2) * D:(n % 2 + 1) * D],
                             ps[n % 2][:, :], AF.Copy)

        @block.tensor
        def _(t):
            cx.wait_bar(t)
            dw(t, ld, 16)
            for ji, (cb, out, mode) in enumerate(jobs):
                dw(t, ld, 16 + (ji + 1) * 16)
                for ot in range(16):
                    n = ji * 16 + ot
                    if n >= 2:
                        t.wait_ge(sss, SS0 + n - 1)
                    p = ps[n % 2]
                    ins = None
                    for c in range(CH):
                        last = (c == CH - 1)
                        for q4 in range(4):
                            if mode == "feat":
                                lhsT = w_sb[:, c * D + ot * 128: c * D + (ot + 1) * 128]
                                rhs = xn_sb[:, c * T + q4 * 512: c * T + (q4 + 1) * 512]
                            else:
                                lhsT = xn_sb[:, c * T + ot * 128: c * T + (ot + 1) * 128]
                                rhs = w_sb[:, c * D + q4 * 512: c * D + (q4 + 1) * 512]
                            ins = t.matmul(p[:, q4 * 512:(q4 + 1) * 512], lhsT, rhs,
                                           start=(c == 0), stop=last)
                    ins.then_inc(mmg, 1)

        @block.gpsimd
        def _(g):
            g = cx.G(g)
            cx.wait_bar(g)
            for c in range(CH):
                g.dma_start(out=xn_sb[:, c * T:(c + 1) * T],
                            in_=xnT[c * 128:(c + 1) * 128, :]).then_inc(ld, 16)
            for ji, (cb, out, mode) in enumerate(jobs):
                if ji > 0:
                    g.wait_ge(mmg, ji * 16)
                for c in range(CH):
                    g.dma_start(out=w_sb[:, c * D:(c + 1) * D],
                                in_=w[c * 128:(c + 1) * 128, cb:cb + D]).then_inc(ld, 16)
                for ot in range(16):
                    g.wait_ge(sss, SS0 + ji * 16 + ot + 1)
                    g.dma_start(out=out[ot * 128:(ot + 1) * 128, :],
                                in_=stage[:, (ot % 2) * D:(ot % 2 + 1) * D]).then_inc(st, 16)
    cx.end_phase([(ld, 64), (mmg, 48), (st, 48)])



def emit_gamma(cx, w1T, w2T, w3T, thr):
    mats = [(w1T, CH, HID), (w2T, CH, HID), (w3T, NHT, D)]
    nslots = CH + CH + NHT
    mats = [(w1T, CH, HID), (w2T, CH, HID), (w3T, NHT, D)]
    nslots = CH + CH + NHT
    nc = cx.nc
    es = ExitStack()
    with es:
        wbuf = [es.enter_context(cx.sbuf(f"wbuf{i}", [128, HID], F32)) for i in range(2)]
        acc = es.enter_context(cx.sbuf("acc", [128, nslots], F32))
        ones_c = es.enter_context(cx.sbuf("ones_c", [128, 1], F32))
        tsb = es.enter_context(cx.sbuf("tsb", [1, 8], F32))
        ps = es.enter_context(cx.psum("ps", [1, nslots], F32))
        ld = cx.dsem("ld")
        tfin = cx.sem("tfin")
        pass
        st = cx.dsem("st")
        block = cx.block
        vss = cx.vss
        sss = cx.sss
        SS0 = cx.sw_n()
        ev = {}

        @block.vector
        def _(v):
            v = cx.V(v)
            cx.wait_bar(v)
            v.memset(ones_c[:, :], 1.0)
            v.memset(tsb[0:1, :], 0.0)
            i = 0
            for (w, nch, cols) in mats:
                for c in range(nch):
                    dw(v, ld, (i + 1))
                    v.tensor_reduce(acc[:, i:i + 1], wbuf[i % 2][:, :cols], AX.X, ALU.add,
                                    apply_absolute_value=True)
                    ev[f"rd_{i}"] = v.n
                    i += 1
            v.wait_ge(tfin, 1)
            base = 0
            for mi, (w, nch, cols) in enumerate(mats):
                nel = float(nch) * 128.0 * float(cols)
                v.tensor_reduce(tsb[0:1, mi:mi + 1], ps[0:1, base:base + nch], AX.X, ALU.add)
                v.tensor_scalar(tsb[0:1, mi:mi + 1], tsb[0:1, mi:mi + 1], 1.0 / nel, Q_EPS,
                                ALU.mult, ALU.add)
                v.tensor_scalar_mul(tsb[0:1, mi:mi + 1], tsb[0:1, mi:mi + 1], 0.5)
                base += nch
            for mi in range(3):
                v.tensor_scalar_mul(tsb[0:1, 4 + mi:5 + mi], tsb[0:1, mi:mi + 1], -1.0)
            ev["fin"] = v.n

        @block.tensor
        def _(t):
            cx.wait_bar(t)
            t.wait_ge(vss, ev[f"rd_{nslots - 1}"])
            t.matmul(ps[0:1, :], ones_c[:, 0:1], acc[:, :], start=True, stop=True).then_inc(tfin, 1)

        @block.gpsimd
        def _(g):
            g = cx.G(g)
            cx.wait_bar(g)
            i = 0
            for (w, nch, cols) in mats:
                for c in range(nch):
                    if i >= 2:
                        g.wait_ge(vss, ev[f"rd_{i - 2}"])
                    g.dma_start(out=wbuf[i % 2][:, :cols],
                                in_=w[c * 128:(c + 1) * 128, :]).then_inc(ld, 16)
                    i += 1
            g.wait_ge(vss, ev["fin"])
            g.dma_start(out=thr, in_=tsb[0:1, :]).then_inc(st, 16)
    cx.end_phase([(ld, 76), (tfin, 1), (st, 1)])



def emit_ffna(cx, xnT, w1T, w2T, thr, hid):
    nc = cx.nc
    es = ExitStack()
    with es:
        xn_sb = es.enter_context(cx.sbuf("xn_sb", [128, CH * T], BF16))
        thr_b = es.enter_context(cx.sbuf("thr_b", [128, 8], F32))
        w1b = [es.enter_context(cx.sbuf(f"w1b{i}", [128, D], F32)) for i in range(2)]
        w2b = [es.enter_context(cx.sbuf(f"w2b{i}", [128, D], F32)) for i in range(2)]
        q1 = es.enter_context(cx.sbuf("q1", [128, D], BF16))
        q2 = es.enter_context(cx.sbuf("q2", [128, D], BF16))
        qa = es.enter_context(cx.sbuf("qa", [128, D], BF16))
        qb = es.enter_context(cx.sbuf("qb", [128, D], BF16))
        sig = es.enter_context(cx.sbuf("sig", [128, T], BF16))
        hsb = [es.enter_context(cx.sbuf(f"hsb{i}", [128, T], BF16)) for i in range(2)]
        ld = cx.dsem("ld")
        mmd = cx.sem("mmd")
        pass
        pass
        st = cx.dsem("st")
        ps1 = es.enter_context(cx.psum("ps1", [128, T], F32))
        ps2 = es.enter_context(cx.psum("ps2", [128, T], F32))
        block = cx.block
        vss = cx.vss
        sss = cx.sss
        SS0 = cx.sw_n()
        ev = {}

        @block.vector
        def _(v):
            v = cx.V(v)
            cx.wait_bar(v)
            for ht in range(NHT):
                dw(v, ld, (17 + 2 * (ht + 1)))
                if ht >= 1:
                    v.wait_ge(mmd, ht)
                v.tensor_scalar(qa[:, :], w1b[ht % 2][:, :], thr_b[:, 0:1], None, ALU.is_gt)
                v.tensor_scalar(qb[:, :], w1b[ht % 2][:, :], thr_b[:, 4:5], None, ALU.is_lt)
                v.tensor_sub(q1[:, :], qa[:, :], qb[:, :])
                v.tensor_scalar(qa[:, :], w2b[ht % 2][:, :], thr_b[:, 1:2], None, ALU.is_gt)
                v.tensor_scalar(qb[:, :], w2b[ht % 2][:, :], thr_b[:, 5:6], None, ALU.is_lt)
                v.tensor_sub(q2[:, :], qa[:, :], qb[:, :])
                ev[f"qd_{ht}"] = v.n
                v.wait_ge(sss, SS0 + ht + 1)
                if ht >= 2:
                    dw(v, st, (ht - 1))
                v.tensor_tensor(hsb[ht % 2][:, :], sig[:, :], ps2[:, :], ALU.mult)
                ev[f"hdn_{ht}"] = v.n

        @block.scalar
        def _(s):
            s = cx.S(s)
            cx.wait_bar(s)
            for ht in range(NHT):
                s.wait_ge(mmd, ht + 1)
                if ht >= 1:
                    s.wait_ge(vss, ev[f"hdn_{ht - 1}"])
                s.activation(sig[:, :], ps1[:, :], AF.Sigmoid)

        @block.tensor
        def _(t):
            cx.wait_bar(t)
            dw(t, ld, 17)
            for ht in range(NHT):
                t.wait_ge(vss, ev[f"qd_{ht}"])
                if ht >= 1:
                    t.wait_ge(sss, SS0 + ht)
                    t.wait_ge(vss, ev[f"hdn_{ht - 1}"])
                ins = None
                for c in range(CH):
                    last = (c == CH - 1)
                    l1 = q1[:, c * 128:(c + 1) * 128]
                    l2 = q2[:, c * 128:(c + 1) * 128]
                    for q4 in range(4):
                        rhs = xn_sb[:, c * T + q4 * 512: c * T + (q4 + 1) * 512]
                        t.matmul(ps1[:, q4 * 512:(q4 + 1) * 512], l1, rhs, start=(c == 0), stop=last)
                    for q4 in range(4):
                        rhs = xn_sb[:, c * T + q4 * 512: c * T + (q4 + 1) * 512]
                        ins = t.matmul(ps2[:, q4 * 512:(q4 + 1) * 512], l2, rhs, start=(c == 0), stop=last)
                ins.then_inc(mmd, 1)

        @block.gpsimd
        def _(g):
            g = cx.G(g)
            cx.wait_bar(g)
            g.dma_start(out=thr_b[:, :], in_=thr.broadcast_to([128, 8])).then_inc(ld, 16)
            for c in range(CH):
                g.dma_start(out=xn_sb[:, c * T:(c + 1) * T],
                            in_=xnT[c * 128:(c + 1) * 128, :]).then_inc(ld, 16)
            for ht in range(NHT):
                if ht >= 2:
                    g.wait_ge(vss, ev[f"qd_{ht - 1}"])
                g.dma_start(out=w1b[ht % 2][:, :].rearrange("p (c h) -> p c h", h=128),
                            in_=w1T[:, ht * 128:(ht + 1) * 128].rearrange("(c p) h -> p c h", p=128)
                            ).then_inc(ld, 16)
                g.dma_start(out=w2b[ht % 2][:, :].rearrange("p (c h) -> p c h", h=128),
                            in_=w2T[:, ht * 128:(ht + 1) * 128].rearrange("(c p) h -> p c h", p=128)
                            ).then_inc(ld, 16)
                g.wait_ge(vss, ev[f"hdn_{ht}"])
                g.dma_start(out=hid[ht * 128:(ht + 1) * 128, :],
                            in_=hsb[ht % 2][:, :]).then_inc(st, 16)
    cx.end_phase([(ld, 105), (mmd, NHT), (st, NHT)])



def emit_ffnb(cx, hid, w3T, thr, xT, aT, x1T):
    NG, GS = 4, 11
    NG, GS = 4, 11
    nc = cx.nc
    es = ExitStack()
    with es:
        y_sb = es.enter_context(cx.sbuf("y_sb", [128, CH * 1024], F32))
        hbuf = es.enter_context(cx.sbuf("hbuf", [128, GS * 1024], BF16))
        wbuf = [es.enter_context(cx.sbuf(f"wbuf{i}", [128, D], F32)) for i in range(2)]
        q3 = es.enter_context(cx.sbuf("q3", [128, GS * D], BF16))
        qa = es.enter_context(cx.sbuf("qa", [128, D], BF16))
        qb = es.enter_context(cx.sbuf("qb", [128, D], BF16))
        thr_b = es.enter_context(cx.sbuf("thr_b", [128, 8], F32))
        xc = es.enter_context(cx.sbuf("xc", [128, 1024], F32))
        ac = es.enter_context(cx.sbuf("ac", [128, 1024], BF16))
        o_sb = es.enter_context(cx.sbuf("o_sb", [128, 1024], F32))
        psy = [es.enter_context(cx.psum(f"psy{i}", [128, 1024], F32)) for i in range(2)]
        ld = cx.dsem("ld")
        cld = cx.dsem("cld")
        mmd = cx.sem("mmd")
        pass
        st = cx.dsem("st")
        block = cx.block
        vss = cx.vss
        sss = cx.sss
        SS0 = cx.sw_n()
        ev = {}

        @block.vector
        def _(v):
            v = cx.V(v)
            cx.wait_bar(v)
            for th in range(2):
                for grp in range(NG):
                    gi = th * NG + grp
                    if gi >= 1:
                        v.wait_ge(mmd, gi * 16)
                    for i in range(GS):
                        dw(v, ld, (1 + gi * 2 * GS + 2 * (i + 1)))
                        v.tensor_scalar(qa[:, :], wbuf[i % 2][:, :], thr_b[:, 2:3], None, ALU.is_gt)
                        v.tensor_scalar(qb[:, :], wbuf[i % 2][:, :], thr_b[:, 6:7], None, ALU.is_lt)
                        v.tensor_sub(q3[:, i * D:(i + 1) * D], qa[:, :], qb[:, :])
                        ev[f"qc_{gi}_{i}"] = v.n
                    ev[f"qg_{gi}"] = v.n
                    for dt in range(16):
                        n = gi * 16 + dt
                        v.wait_ge(mmd, n + 1)
                        dsl = y_sb[:, dt * 1024:(dt + 1) * 1024]
                        if grp == 0:
                            v.tensor_copy(dsl, psy[n % 2][:, :])
                        else:
                            v.tensor_add(dsl, dsl, psy[n % 2][:, :])
                        ev[f"fl_{n}"] = v.n
                for dc in range(CH):
                    n = th * CH + dc
                    dw(v, cld, 2 * (n + 1))
                    if n > 0:
                        dw(v, st, n)
                    dsl = y_sb[:, dc * 1024:(dc + 1) * 1024]
                    v.tensor_add(o_sb[:, :], dsl, ac[:, :])
                    v.tensor_add(o_sb[:, :], o_sb[:, :], xc[:, :])
                    ev[f"od_{n}"] = v.n

        @block.tensor
        def _(t):
            cx.wait_bar(t)
            for th in range(2):
                for grp in range(NG):
                    gi = th * NG + grp
                    t.wait_ge(vss, ev[f"qg_{gi}"])
                    for dt in range(16):
                        n = gi * 16 + dt
                        if n >= 2:
                            t.wait_ge(vss, ev[f"fl_{n - 2}"])
                        p = psy[n % 2]
                        ins = None
                        for i in range(GS):
                            lhsT = q3[:, i * D + dt * 128: i * D + (dt + 1) * 128]
                            t.matmul(p[:, 0:512], lhsT, hbuf[:, i * 1024:i * 1024 + 512],
                                     start=(i == 0), stop=(i == GS - 1))
                            ins = t.matmul(p[:, 512:1024], lhsT, hbuf[:, i * 1024 + 512:(i + 1) * 1024],
                                           start=(i == 0), stop=(i == GS - 1))
                        ins.then_inc(mmd, 1)

        @block.gpsimd
        def _(g):
            g = cx.G(g)
            cx.wait_bar(g)
            g.dma_start(out=thr_b[:, :], in_=thr.broadcast_to([128, 8])).then_inc(ld, 16)
            for th in range(2):
                tcols = slice(th * 1024, (th + 1) * 1024)
                for grp in range(NG):
                    gi = th * NG + grp
                    if gi > 0:
                        g.wait_ge(mmd, gi * 16)
                    for i in range(GS):
                        hc = grp * GS + i
                        g.dma_start(out=hbuf[:, i * 1024:(i + 1) * 1024],
                                    in_=hid[hc * 128:(hc + 1) * 128, tcols]).then_inc(ld, 16)
                        nq = gi * GS + i
                        if nq >= 2:
                            g.wait_ge(vss, ev[f"qc_{(nq - 2) // GS}_{(nq - 2) % GS}"])
                        g.dma_start(out=wbuf[i % 2][:, :],
                                    in_=w3T[hc * 128:(hc + 1) * 128, :]).then_inc(ld, 16)
                for dc in range(CH):
                    n = th * CH + dc
                    if n > 0:
                        g.wait_ge(vss, ev[f"od_{n - 1}"])
                    g.dma_start(out=xc[:, :], in_=xT[dc * 128:(dc + 1) * 128, tcols]).then_inc(cld, 16)
                    g.dma_start(out=ac[:, :], in_=aT[dc * 128:(dc + 1) * 128, tcols]).then_inc(cld, 16)
                    g.wait_ge(vss, ev[f"od_{n}"])
                    g.dma_start(out=x1T[dc * 128:(dc + 1) * 128, tcols], in_=o_sb[:, :]).then_inc(st, 16)
    cx.end_phase([(ld, 177), (cld, 64), (mmd, 128), (st, 32)])



def emit_attn0(cx, q_tok, gk, gv, maskabT, identb, attn_tok):
    nc = cx.nc
    es = ExitStack()
    with es:
        idn = es.enter_context(cx.sbuf("idn", [128, 128], BF16))
        mskT = es.enter_context(cx.sbuf("mskT", [128, 256], F32))
        qbuf = [es.enter_context(cx.sbuf(f"qbuf{i}", [128, D], BF16)) for i in range(2)]
        qt_sb = es.enter_context(cx.sbuf("qt_sb", [128, H * 128], BF16))
        kb = [es.enter_context(cx.sbuf(f"kb{i}", [128, 2 * T], BF16)) for i in range(4)]
        vb = [es.enter_context(cx.sbuf(f"vb{i}", [128, 32 * 129], BF16)) for i in range(4)]
        p_sb = [es.enter_context(cx.sbuf(f"p_sb{i}", [128, 4096], BF16)) for i in range(2)]
        rs = [es.enter_context(cx.sbuf(f"rs{i}", [128, 1], F32)) for i in range(2)]
        ostage = [es.enter_context(cx.sbuf(f"ostage{i}", [128, D], BF16)) for i in range(2)]
        ps_pt = [es.enter_context(cx.psum(f"ps_pt{i}", [128, 1024], BF16)) for i in range(2)]
        ps_s = [es.enter_context(cx.psum(f"ps_s{i}", [128, 512], F32)) for i in range(4)]
        ps_o = [es.enter_context(cx.psum(f"ps_o{i}", [128, 129], F32)) for i in range(2)]
        ld = cx.dsem("ld", w=6)
        trd = cx.sem("trd")
        mmd = cx.sem("mmd")
        avd = cx.sem("avd")
        st = cx.dsem("st")
        block = cx.block
        vss = cx.vss
        sss = cx.sss
        ev = {}

        chunks = []
        for j in range(NT):
            tl = 2 * EXT[j]
            cs = []
            t0 = 0
            while t0 < tl:
                tn = min(4, tl - t0)
                cs.append((t0, tn))
                t0 += tn
            chunks.append(cs)
        nch_t = [len(chunks[j]) for j in range(NT)]
        mm_first = []
        c = 0
        for j in range(NT):
            for h in range(H):
                mm_first.append(c)
                c += nch_t[j]
        nmm_total = c
        diag = []
        for j in range(NT):
            dd = {}
            for par, dtile in ((0, EXT[j] - 1), (1, 2 * EXT[j] - 1)):
                ci = dtile // 4
                dd.setdefault(ci, []).append((dtile % 4, par))
            diag.append(dd)
        ld_q, ld_kv = [], []
        n = 2
        for j in range(NT):
            n += 1
            ld_q.append(n)
            row = []
            for h in range(H):
                n += 3
                row.append(n)
            ld_kv.append(row)

        @block.vector
        def _(v):
            v = cx.V(v)
            cx.wait_bar(v)
            for i in range(4):
                v.memset(vb[i].rearrange("p (t c) -> p t c", c=129)[:, :, 128:129], 1.0)

            def lag_v(pidx):
                pj, phh = divmod(pidx, H)
                v.wait_ge(avd, pidx + 1)
                if phh == 0 and pj >= 2:
                    dw(v, st, (pj - 1))
                v.reciprocal(rs[pidx % 2][:, 0:1], ps_o[pidx % 2][:, 128:129])
                v.tensor_scalar(ostage[pj % 2][:, phh * 128:(phh + 1) * 128],
                                ps_o[pidx % 2][:, 0:128], rs[pidx % 2][:, 0:1],
                                None, ALU.mult)
                ev[f"osd_{pidx}"] = v.n

            for j in range(NT):
                v.wait_ge(trd, 2 * (j + 1))
                v.tensor_copy(qt_sb[:, 0:1024], ps_pt[0][:, :])
                v.tensor_copy(qt_sb[:, 1024:2048], ps_pt[1][:, :])
                ev[f"qtev_{j}"] = v.n
                for h in range(H):
                    idx = j * H + h
                    for ci, tiles in sorted(diag[j].items()):
                        gch = mm_first[idx] + ci
                        v.wait_ge(mmd, gch + 1)
                        for (tloc, par) in tiles:
                            v.tensor_add(ps_s[gch % 4][:, tloc * 128:(tloc + 1) * 128],
                                         ps_s[gch % 4][:, tloc * 128:(tloc + 1) * 128],
                                         mskT[:, par * 128:(par + 1) * 128])
                        ev[f"mskd_{idx}_{ci}"] = v.n
                    if idx >= 1:
                        lag_v(idx - 1)
            lag_v(NT * H - 1)

        @block.scalar
        def _(s):
            s = cx.S(s)
            cx.wait_bar(s)
            nmm = 0
            for j in range(NT):
                for h in range(H):
                    idx = j * H + h
                    for ci, (t0, tn) in enumerate(chunks[j]):
                        s.wait_ge(mmd, nmm + 1)
                        if ci in diag[j]:
                            s.wait_ge(vss, ev[f"mskd_{idx}_{ci}"])
                        if ci == 0 and idx >= 2:
                            s.wait_ge(avd, idx - 1)
                        s.activation(p_sb[idx % 2][:, t0 * 128:(t0 + tn) * 128],
                                     ps_s[nmm % 4][:, 0:tn * 128], AF.Exp, scale=SCALE)
                        ev[f"sev_{nmm}"] = s.n
                        nmm += 1

        @block.tensor
        def _(t):
            cx.wait_bar(t)
            nmm = 0

            def lag_av(pidx):
                pj = pidx // H
                tl = 2 * EXT[pj]
                t.wait_ge(sss, ev[f"sev_{mm_first[pidx] + nch_t[pj] - 1}"])
                if pidx >= 2:
                    t.wait_ge(vss, ev[f"osd_{pidx - 2}"])
                ins = None
                for kt in range(tl):
                    ins = t.matmul(ps_o[pidx % 2][:, 0:129],
                                   p_sb[pidx % 2][:, kt * 128:(kt + 1) * 128],
                                   vb[pidx % 4][:, kt * 129:kt * 129 + 129],
                                   start=(kt == 0), stop=(kt == tl - 1))
                ins.then_inc(avd, 1)

            for j in range(NT):
                dw(t, ld, ld_q[j])
                if j > 0:
                    t.wait_ge(vss, ev[f"qtev_{j - 1}"])
                for half in range(2):
                    ins = None
                    for hh in range(8):
                        h8 = half * 8 + hh
                        ins = t.transpose(ps_pt[half][:, hh * 128:(hh + 1) * 128],
                                          qbuf[j % 2][:, h8 * 128:(h8 + 1) * 128], idn[:, :])
                    ins.then_inc(trd, 1)
                t.wait_ge(vss, ev[f"qtev_{j}"])
                for h in range(H):
                    idx = j * H + h
                    dw(t, ld, ld_kv[j][h])
                    for ci, (t0, tn) in enumerate(chunks[j]):
                        if nmm >= 4:
                            t.wait_ge(sss, ev[f"sev_{nmm - 4}"])
                        ins = None
                        for k in range(tn):
                            kt = t0 + k
                            ins = t.matmul(ps_s[nmm % 4][:, k * 128:(k + 1) * 128],
                                           kb[idx % 4][:, kt * 128:(kt + 1) * 128],
                                           qt_sb[:, h * 128:(h + 1) * 128],
                                           start=True, stop=True)
                        ins.then_inc(mmd, 1)
                        nmm += 1
                    if h >= 1:
                        lag_av(idx - 1)
                lag_av(j * H + H - 1)

        @block.gpsimd
        def _(g):
            g = cx.G(g)
            cx.wait_bar(g)
            g.dma_start(out=idn[:, :], in_=identb).then_inc(ld, 16)
            g.dma_start(out=mskT[:, :].rearrange("p (a k) -> p a k", k=128),
                        in_=maskabT.rearrange("(a p) k -> p a k", p=128)).then_inc(ld, 16)
            for j in range(NT):
                w = EXT[j] * 128
                if j >= 2:
                    g.wait_ge(trd, 2 * (j - 1))
                g.dma_start(out=qbuf[j % 2][:, :], in_=q_tok[j * 128:(j + 1) * 128, :]).then_inc(ld, 16)
                for h in range(H):
                    idx = j * H + h
                    if idx >= 4:
                        g.wait_ge(avd, idx - 3)
                    g.dma_start(
                        out=kb[idx % 4][:, 0:2 * w].rearrange("p (a w) -> p a w", a=2),
                        in_=gk.rearrange("(a r) t -> a r t", a=2)[:, h * 128:(h + 1) * 128, 0:w]
                        .rearrange("a p w -> p a w"),
                    ).then_inc(ld, 16)
                    for par in range(2):
                        g.dma_start(
                            out=vb[idx % 4][:, par * EXT[j] * 129:(par + 1) * EXT[j] * 129]
                            .rearrange("p (t c) -> p t c", c=129)[:, :, 0:128],
                            in_=gv[par * T: par * T + w, h * 128:(h + 1) * 128]
                            .rearrange("(t p) d -> p t d", p=128),
                        ).then_inc(ld, 16)

        @block.sync
        def _(sy):
            sy = cx.SY(sy)
            cx.wait_bar(sy)
            for j in range(NT):
                sy.wait_ge(vss, ev[f"osd_{(j + 1) * H - 1}"])
                sy.dma_start(out=attn_tok[j * 128:(j + 1) * 128, :],
                             in_=ostage[j % 2][:, :]).then_inc(st, 16)
    cx.end_phase([(ld, ld_kv[NT - 1][H - 1]), (trd, 2 * NT), (mmd, nmm_total),
                  (avd, NT * H), (st, NT)])


def emit_oproj(cx, attn_tok, w, aT):
    nc = cx.nc
    es = ExitStack()
    with es:
        at_sb = es.enter_context(cx.sbuf("at_sb", [128, CH * T], BF16))
        w_sb = es.enter_context(cx.sbuf("w_sb", [128, CH * D], BF16))
        stage = es.enter_context(cx.sbuf("stage", [128, 2 * T], BF16))
        ps = [es.enter_context(cx.psum(f"ps{i}", [128, T], F32)) for i in range(2)]
        ld = cx.dsem("ld")
        tld = cx.dsem("tld")
        mmg = cx.sem("mmg")
        pass
        st = cx.dsem("st")
        block = cx.block
        vss = cx.vss
        sss = cx.sss
        SS0 = cx.sw_n()

        @block.scalar
        def _(s):
            s = cx.S(s)
            cx.wait_bar(s)
            for ot in range(16):
                s.wait_ge(mmg, ot + 1)
                if ot >= 2:
                    dw(s, st, (ot - 1))
                s.activation(stage[:, (ot % 2) * T:(ot % 2 + 1) * T],
                             ps[ot % 2][:, :], AF.Copy)

        @block.tensor
        def _(t):
            cx.wait_bar(t)
            dw(t, ld, 16)
            dw(t, tld, 16)
            for ot in range(16):
                if ot >= 2:
                    t.wait_ge(sss, SS0 + ot - 1)
                p = ps[ot % 2]
                ins = None
                for c in range(CH):
                    lhsT = w_sb[:, c * D + ot * 128: c * D + (ot + 1) * 128]
                    for q4 in range(4):
                        ins = t.matmul(p[:, q4 * 512:(q4 + 1) * 512], lhsT,
                                       at_sb[:, c * T + q4 * 512: c * T + (q4 + 1) * 512],
                                       start=(c == 0), stop=(c == CH - 1))
                ins.then_inc(mmg, 1)

        @block.sync
        def _(sy):
            sy = cx.SY(sy)
            cx.wait_bar(sy)
            for c in range(CH):
                sy.dma_start(out=at_sb[:, c * T:(c + 1) * T],
                             in_=attn_tok[:, c * 128:(c + 1) * 128], transpose=True).then_inc(tld, 16)

        @block.gpsimd
        def _(g):
            g = cx.G(g)
            cx.wait_bar(g)
            for c in range(CH):
                g.dma_start(out=w_sb[:, c * D:(c + 1) * D],
                            in_=w[c * 128:(c + 1) * 128, :]).then_inc(ld, 16)
            for ot in range(16):
                g.wait_ge(sss, SS0 + ot + 1)
                g.dma_start(out=aT[ot * 128:(ot + 1) * 128, :],
                            in_=stage[:, (ot % 2) * T:(ot % 2 + 1) * T]).then_inc(st, 16)
    cx.end_phase([(ld, 16), (tld, 16), (mmg, 16), (st, 16)])



def emit_attn1b1(cx, q_tok, gk, maskab, identb, pt_dram):
    nc = cx.nc
    es = ExitStack()
    with es:
        idn = es.enter_context(cx.sbuf("idn", [128, 128], BF16))
        msk = es.enter_context(cx.sbuf("msk", [128, 256], F32))
        qbuf = [es.enter_context(cx.sbuf(f"qbuf{i}", [128, D], BF16)) for i in range(2)]
        qt_sb = es.enter_context(cx.sbuf("qt_sb", [128, H * 128], BF16))
        kb = [es.enter_context(cx.sbuf(f"kb{i}", [128, 2 * T], BF16)) for i in range(4)]
        sc_sb = [es.enter_context(cx.sbuf(f"sc_sb{i}", [128, 4096], F32)) for i in range(2)]
        p_sb = [es.enter_context(cx.sbuf(f"p_sb{i}", [128, 4096], BF16)) for i in range(2)]
        pa_sb = [es.enter_context(cx.sbuf(f"pa_sb{i}", [128, 4096], F32)) for i in range(2)]
        nm = [es.enter_context(cx.sbuf(f"nm{i}", [128, 1], F32)) for i in range(2)]
        ssum = [es.enter_context(cx.sbuf(f"ssum{i}", [128, 1], F32)) for i in range(2)]
        rs = [es.enter_context(cx.sbuf(f"rs{i}", [128, 1], F32)) for i in range(2)]
        ps_pt = es.enter_context(cx.psum("ps_pt", [128, H * 128], BF16))
        ps_s = [es.enter_context(cx.psum(f"ps_s{i}", [128, 512], F32)) for i in range(4)]
        ld = cx.dsem("ld", w=6)
        trd = cx.sem("trd")
        mmd = cx.sem("mmd")
        st = cx.dsem("st")
        block = cx.block
        vss = cx.vss
        sss = cx.sss
        SS0 = cx.sw_n()
        ev = {}

        CKS = [_chunks(EXT[j] * 128) for j in range(NT)]
        nch = [2 * len(CKS[j]) for j in range(NT)]
        ld_q, ld_kv = [], []
        n = 2
        for j in range(NT):
            n += 1
            ld_q.append(n)
            row = []
            for h in range(H):
                n += 1
                row.append(n)
            ld_kv.append(row)
        sss_afterE = [0] * (NT * H)
        sss_afterX = [0] * (NT * H)
        c = 0
        for idx in range(NT * H):
            c += nch[idx // H]
            sss_afterE[idx] = c
            if idx >= 1:
                c += 1
                sss_afterX[idx - 1] = c
        c += 1
        sss_afterX[NT * H - 1] = c
        mm_idx = []
        c = 0
        for j in range(NT):
            for h in range(H):
                c += nch[j]
                mm_idx.append(c)

        @block.vector
        def _(v):
            v = cx.V(v)
            cx.wait_bar(v)

            def lag(pidx):
                pj, phh = divmod(pidx, H)
                pw = EXT[pj] * 128
                v.wait_ge(sss, SS0 + sss_afterX[pidx])
                v.reciprocal(rs[pidx % 2][:, 0:1], ssum[pidx % 2][:, 0:1])
                if phh == 0 and pj >= 2:
                    dw(v, st, (pj - 1))
                pa = pa_sb[pj % 2]
                if phh == 0:
                    v.tensor_scalar(pa[:, 0:2 * pw], p_sb[pidx % 2][:, 0:2 * pw],
                                    rs[pidx % 2][:, 0:1], None, ALU.mult)
                else:
                    v.scalar_tensor_tensor(pa[:, 0:2 * pw], p_sb[pidx % 2][:, 0:2 * pw],
                                           rs[pidx % 2][:, 0:1], pa[:, 0:2 * pw],
                                           ALU.mult, ALU.add)
                ev[f"acd_{pidx}"] = v.n

            for j in range(NT):
                w = EXT[j] * 128
                v.wait_ge(trd, j + 1)
                v.tensor_copy(qt_sb[:, :], ps_pt[:, :])
                ev[f"qtev_{j}"] = v.n
                for h in range(H):
                    idx = j * H + h
                    v.wait_ge(sss, SS0 + sss_afterE[idx])
                    sc = sc_sb[idx % 2]
                    dt0 = slice((EXT[j] - 1) * 128, EXT[j] * 128)
                    v.tensor_add(sc[:, dt0], sc[:, dt0], msk[:, 0:128])
                    dt1 = slice(w + (EXT[j] - 1) * 128, w + EXT[j] * 128)
                    v.tensor_add(sc[:, dt1], sc[:, dt1], msk[:, 128:256])
                    v.tensor_reduce(nm[idx % 2][:, 0:1], sc[:, 0:2 * w], AX.X, ALU.max, negate=True)
                    v.tensor_scalar_mul(nm[idx % 2][:, 0:1], nm[idx % 2][:, 0:1], SCALE)
                    ev[f"stats_{idx}"] = v.n
                    if idx >= 1:
                        lag(idx - 1)
            lag(NT * H - 1)

        @block.scalar
        def _(s):
            s = cx.S(s)
            cx.wait_bar(s)
            nmm = 0

            def do_exp(pidx):
                pw = EXT[pidx // H] * 128
                s.wait_ge(vss, ev[f"stats_{pidx}"])
                if pidx >= 2:
                    s.wait_ge(vss, ev[f"acd_{pidx - 2}"])
                s.activation(p_sb[pidx % 2][:, 0:2 * pw], sc_sb[pidx % 2][:, 0:2 * pw],
                             AF.Exp, scale=SCALE, bias=nm[pidx % 2][:, 0:1],
                             accum_out=ssum[pidx % 2][:, 0:1])

            for j in range(NT):
                w = EXT[j] * 128
                for h in range(H):
                    idx = j * H + h
                    sc = sc_sb[idx % 2]
                    for par in range(2):
                        for (off, cw) in CKS[j]:
                            s.wait_ge(mmd, nmm + 1)
                            s.activation(sc[:, par * w + off: par * w + off + cw],
                                         ps_s[nmm % 4][:, 0:cw], AF.Copy)
                            ev[f"sev_{nmm}"] = s.n
                            nmm += 1
                    if idx >= 1:
                        do_exp(idx - 1)
            do_exp(NT * H - 1)

        @block.tensor
        def _(t):
            cx.wait_bar(t)
            nmm = 0
            for j in range(NT):
                w = EXT[j] * 128
                dw(t, ld, ld_q[j])
                if j > 0:
                    t.wait_ge(vss, ev[f"qtev_{j - 1}"])
                ins = None
                for h in range(H):
                    ins = t.transpose(ps_pt[:, h * 128:(h + 1) * 128],
                                      qbuf[j % 2][:, h * 128:(h + 1) * 128], idn[:, :])
                ins.then_inc(trd, 1)
                t.wait_ge(vss, ev[f"qtev_{j}"])
                for h in range(H):
                    idx = j * H + h
                    dw(t, ld, ld_kv[j][h])
                    for par in range(2):
                        for (off, cw) in CKS[j]:
                            if nmm >= 4:
                                t.wait_ge(sss, ev[f"sev_{nmm - 4}"])
                            t.matmul(ps_s[nmm % 4][:, 0:cw],
                                     qt_sb[:, h * 128:(h + 1) * 128],
                                     kb[idx % 4][:, par * w + off: par * w + off + cw],
                                     start=True, stop=True).then_inc(mmd, 1)
                            nmm += 1

        @block.gpsimd
        def _(g):
            g = cx.G(g)
            cx.wait_bar(g)
            g.dma_start(out=idn[:, :], in_=identb).then_inc(ld, 16)
            g.dma_start(out=msk[:, :].rearrange("p (a k) -> p a k", k=128),
                        in_=maskab.rearrange("(a p) k -> p a k", p=128)).then_inc(ld, 16)
            for j in range(NT):
                w = EXT[j] * 128
                if j >= 2:
                    g.wait_ge(trd, j - 1)
                g.dma_start(out=qbuf[j % 2][:, :], in_=q_tok[j * 128:(j + 1) * 128, :]).then_inc(ld, 16)
                for h in range(H):
                    idx = j * H + h
                    if idx >= 4:
                        g.wait_ge(mmd, mm_idx[idx - 4])
                    g.dma_start(
                        out=kb[idx % 4][:, 0:2 * w].rearrange("p (a w) -> p a w", a=2),
                        in_=gk.rearrange("(a r) t -> a r t", a=2)[:, h * 128:(h + 1) * 128, 0:w]
                        .rearrange("a p w -> p a w"),
                    ).then_inc(ld, 16)
                if j >= 1:
                    pw = EXT[j - 1] * 128
                    g.wait_ge(vss, ev[f"acd_{j * H - 1}"])
                    g.dma_start(out=pt_dram[(j - 1) * 128:j * 128, 0:2 * pw],
                                in_=pa_sb[(j - 1) % 2][:, 0:2 * pw]).then_inc(st, 16)
            g.wait_ge(vss, ev[f"acd_{NT * H - 1}"])
            g.dma_start(out=pt_dram[(NT - 1) * 128:NT * 128, 0:2 * EXT[NT - 1] * 128],
                        in_=pa_sb[(NT - 1) % 2][:, 0:2 * EXT[NT - 1] * 128]).then_inc(st, 16)

    nmm_total = sum(16 * 2 * len(_chunks(EXT[j] * 128)) for j in range(NT))
    cx.end_phase([(ld, ld_kv[NT - 1][H - 1]), (trd, NT), (mmd, nmm_total), (st, NT)])


def emit_attn1b2(cx, pt_dram, gv, identb, attn_tok):
    nc = cx.nc
    es = ExitStack()
    with es:
        idn = es.enter_context(cx.sbuf("idn", [128, 128], BF16))
        pbuf = [es.enter_context(cx.sbuf(f"pbuf{i}", [128, 4096], BF16)) for i in range(2)]
        pt_sb = es.enter_context(cx.sbuf("pt_sb", [128, 4096], BF16))
        vbuf = [es.enter_context(cx.sbuf(f"vbuf{i}", [128, D], BF16)) for i in range(2)]
        ostage = [es.enter_context(cx.sbuf(f"ostage{i}", [128, D], BF16)) for i in range(2)]
        ps_pt = es.enter_context(cx.psum("ps_pt", [128, 4096], BF16))
        ps_o = es.enter_context(cx.psum("ps_o", [128, D], F32))
        ld = cx.dsem("ld")
        vld = cx.dsem("vld")
        mmv = cx.sem("mmv")
        trd = cx.sem("trd")
        pass
        st = cx.dsem("st")
        block = cx.block
        vss = cx.vss
        sss = cx.sss
        SS0 = cx.sw_n()
        ev = {}

        cum_mmv = []
        tot = 0
        for j in range(NT):
            tot += 2 * EXT[j]
            cum_mmv.append(tot)

        @block.vector
        def _(v):
            v = cx.V(v)
            cx.wait_bar(v)
            for j in range(NT):
                w = EXT[j] * 128
                v.wait_ge(trd, j + 1)
                v.tensor_copy(pt_sb[:, 0:2 * w], ps_pt[:, 0:2 * w])
                ev[f"ptev_{j}"] = v.n
                v.wait_ge(mmv, cum_mmv[j])
                if j >= 2:
                    dw(v, st, (j - 1))
                v.tensor_copy(ostage[j % 2][:, :], ps_o[:, :])
                ev[f"osd_{j}"] = v.n

        @block.tensor
        def _(t):
            cx.wait_bar(t)
            nv = 0
            for j in range(NT):
                w = EXT[j] * 128
                dw(t, ld, (j + 2))
                if j > 0:
                    t.wait_ge(vss, ev[f"ptev_{j - 1}"])
                ins = None
                for kt in range(2 * EXT[j]):
                    ins = t.transpose(ps_pt[:, kt * 128:(kt + 1) * 128],
                                      pbuf[j % 2][:, kt * 128:(kt + 1) * 128], idn[:, :])
                ins.then_inc(trd, 1)
                t.wait_ge(vss, ev[f"ptev_{j}"])
                if j > 0:
                    t.wait_ge(vss, ev[f"osd_{j - 1}"])
                nkt = 2 * EXT[j]
                for kt in range(nkt):
                    dw(t, vld, (nv + 1))
                    ins = None
                    for q4 in range(4):
                        ins = t.matmul(ps_o[:, q4 * 512:(q4 + 1) * 512],
                                       pt_sb[:, kt * 128:(kt + 1) * 128],
                                       vbuf[nv % 2][:, q4 * 512:(q4 + 1) * 512],
                                       start=(kt == 0), stop=(kt == nkt - 1))
                    ins.then_inc(mmv, 1)
                    nv += 1

        @block.sync
        def _(sy):
            sy = cx.SY(sy)
            cx.wait_bar(sy)
            nv = 0
            for j in range(NT):
                for kt in range(2 * EXT[j]):
                    par, tt = (0, kt) if kt < EXT[j] else (1, kt - EXT[j])
                    if nv >= 2:
                        sy.wait_ge(mmv, nv - 1)
                    sy.dma_start(out=vbuf[nv % 2][:, :],
                                 in_=gv[par * T + tt * 128: par * T + (tt + 1) * 128, :]
                                 ).then_inc(vld, 16)
                    nv += 1

        @block.gpsimd
        def _(g):
            g = cx.G(g)
            cx.wait_bar(g)
            g.dma_start(out=idn[:, :], in_=identb).then_inc(ld, 16)
            for j in range(NT):
                w = EXT[j] * 128
                if j >= 2:
                    g.wait_ge(trd, j - 1)
                g.dma_start(out=pbuf[j % 2][:, 0:2 * w],
                            in_=pt_dram[j * 128:(j + 1) * 128, 0:2 * w]).then_inc(ld, 16)
                g.wait_ge(vss, ev[f"osd_{j}"])
                g.dma_start(out=attn_tok[j * 128:(j + 1) * 128, :],
                            in_=ostage[j % 2][:, :]).then_inc(st, 16)
    cx.end_phase([(ld, 17), (vld, 272), (mmv, 272), (trd, NT), (st, NT)])



def emit_ffn1(cx, xnT, wbase, bbaseT, a_s, b_s, a_r, b_r, wrout, brout, identf, esel, xT, aT, outT):
    nc = cx.nc
    es = ExitStack()
    with es:
        xn_sb = es.enter_context(cx.sbuf("xn_sb", [128, CH * T], BF16))
        wbb = [es.enter_context(cx.sbuf(f"wbb{i}", [128, CH * 128], BF16)) for i in range(2)]
        bb_sb = es.enter_context(cx.sbuf("bb_sb", [128, 16], F32))
        as_sb = es.enter_context(cx.sbuf("as_sb", [128, CH * R], BF16))
        bs_sb = es.enter_context(cx.sbuf("bs_sb", [16, D], BF16))
        ar_sb = es.enter_context(cx.sbuf("ar_sb", [128, E * CH * R], BF16))
        br_sb = es.enter_context(cx.sbuf("br_sb", [16, E * D], BF16))
        wr_sb = es.enter_context(cx.sbuf("wr_sb", [128, CH * E], BF16))
        bro_b = es.enter_context(cx.sbuf("bro_b", [128, E], F32))
        idf = es.enter_context(cx.sbuf("idf", [128, 128], F32))
        esel_sb = es.enter_context(cx.sbuf("esel_sb", [6, E * 16], F32))
        g_sb = es.enter_context(cx.sbuf("g_sb", [128, E], F32))
        ge_sb = es.enter_context(cx.sbuf("ge_sb", [128, E], F32))
        gn_sb = es.enter_context(cx.sbuf("gn_sb", [128, E], F32))
        m1 = es.enter_context(cx.sbuf("m1", [128, 1], F32))
        s1 = es.enter_context(cx.sbuf("s1", [128, 1], F32))
        r1 = es.enter_context(cx.sbuf("r1", [128, 1], F32))
        sel = es.enter_context(cx.sbuf("sel", [128, E], F32))
        g2 = es.enter_context(cx.sbuf("g2", [128, E], F32))
        wgt = es.enter_context(cx.sbuf("wgt", [128, NT * E], F32))
        wt_sb = es.enter_context(cx.sbuf("wt_sb", [6, T], F32))
        tmp_sb = es.enter_context(cx.sbuf("tmp_sb", [16, T], BF16))
        h_sb = es.enter_context(cx.sbuf("h_sb", [16, E * T], BF16))
        xc = es.enter_context(cx.sbuf("xc", [128, T], F32))
        ac = es.enter_context(cx.sbuf("ac", [128, T], BF16))
        o_sb = es.enter_context(cx.sbuf("o_sb", [128, T], F32))
        ld = cx.dsem("ld")
        wld = cx.dsem("wld")
        cld = cx.dsem("cld")
        rta = cx.sem("rta")
        trd = cx.sem("trd")
        cmm = cx.sem("cmm")
        wbm = cx.sem("wbm")
        ymm = cx.sem("ymm")
        pass
        pass
        st = cx.dsem("st")
        block = cx.block
        vss = cx.vss
        sss = cx.sss
        SS0 = cx.sw_n()
        ev = {}
        NLD = 35
        NGRP = 4 * (1 + E)

        ps_g = es.enter_context(cx.psum("ps_g", [128, NT * E], F32))

        with cx.psum("ps_w", [6, T], F32) as ps_w, \
                nc.psum_tensor("ps_t0", [16, 512], F32) as ps_t0, \
                nc.psum_tensor("ps_t1", [16, 512], F32) as ps_t1:
            ps_t = [ps_t0, ps_t1]

            @block.vector
            def _(v):
                v = cx.V(v)
                cx.wait_bar(v)
                v.wait_ge(rta, 1)
                for tt in range(NT):
                    v.tensor_add(g_sb[:, :], ps_g[:, tt * E:(tt + 1) * E], bro_b[:, :])
                    v.tensor_reduce(m1[:, 0:1], g_sb[:, :], AX.X, ALU.max, negate=True)
                    ev[f"rt1_{tt}"] = v.n
                    v.wait_ge(sss, SS0 + tt + 1)
                    v.reciprocal(r1[:, 0:1], s1[:, 0:1])
                    v.tensor_scalar(gn_sb[:, :], ge_sb[:, :], r1[:, 0:1], None, ALU.mult)
                    v.tensor_reduce(m1[:, 0:1], gn_sb[:, :], AX.X, ALU.max)
                    v.tensor_scalar(sel[:, :], gn_sb[:, :], m1[:, 0:1], None, ALU.is_ge)
                    v.scalar_tensor_tensor(g2[:, :], sel[:, :], -1e9, gn_sb[:, :], ALU.mult, ALU.add)
                    v.tensor_reduce(m1[:, 0:1], g2[:, :], AX.X, ALU.max)
                    v.tensor_scalar(sel[:, :], gn_sb[:, :], m1[:, 0:1], None, ALU.is_ge)
                    v.tensor_mul(wgt[:, tt * E:(tt + 1) * E], gn_sb[:, :], sel[:, :])
                    ev[f"wgt_{tt}"] = v.n
                v.wait_ge(trd, 1)
                v.tensor_copy(wt_sb[0:6, :], ps_w[0:6, :])
                ev["wtev"] = v.n
                for ngrp in range(NGRP):
                    src, q4 = divmod(ngrp, 4)
                    v.wait_ge(cmm, ngrp + 1)
                    if src == 0:
                        dst = tmp_sb[0:16, q4 * 512:(q4 + 1) * 512]
                    else:
                        e = src - 1
                        dst = h_sb[0:16, e * T + q4 * 512: e * T + (q4 + 1) * 512]
                    v.tensor_copy(dst, ps_t[ngrp % 2][0:16, :])
                    ev[f"cev_{ngrp}"] = v.n

            @block.scalar
            def _(s):
                s = cx.S(s)
                cx.wait_bar(s)
                for tt in range(NT):
                    s.wait_ge(vss, ev[f"rt1_{tt}"])
                    s.activation(ge_sb[:, :], g_sb[:, :], AF.Exp, bias=m1[:, 0:1],
                                 accum_out=s1[:, 0:1])

            @block.tensor
            def _(t):
                cx.wait_bar(t)
                dw(t, ld, NLD)
                ins = None
                for tt in range(NT):
                    for c in range(CH):
                        ins = t.matmul(ps_g[:, tt * E:(tt + 1) * E],
                                       xn_sb[:, c * T + tt * 128: c * T + (tt + 1) * 128],
                                       wr_sb[:, c * E:(c + 1) * E],
                                       start=(c == 0), stop=(c == CH - 1))
                ins.then_inc(rta, 1)
                t.wait_ge(vss, ev[f"wgt_{NT - 1}"])
                ins = None
                for tt in range(NT):
                    ins = t.transpose(ps_w[:, tt * 128:(tt + 1) * 128],
                                      wgt[:, tt * E:(tt + 1) * E], idf[:, :])
                ins.then_inc(trd, 1)
                for ngrp in range(NGRP):
                    src, q4 = divmod(ngrp, 4)
                    if ngrp >= 2:
                        t.wait_ge(vss, ev[f"cev_{ngrp - 2}"])
                    ins = None
                    for c in range(CH):
                        if src == 0:
                            lhsT = as_sb[:, c * R:(c + 1) * R]
                        else:
                            e = src - 1
                            lhsT = ar_sb[:, e * CH * R + c * R: e * CH * R + (c + 1) * R]
                        ins = t.matmul(ps_t[ngrp % 2][0:16, :], lhsT,
                                       xn_sb[:, c * T + q4 * 512: c * T + (q4 + 1) * 512],
                                       start=(c == 0), stop=(c == CH - 1))
                    ins.then_inc(cmm, 1)

        with cx.psum("ps_wb", [16, T], F32) as ps_wb:

            @block.vector
            def _(v):
                v = cx.V(v)
                for e in range(E):
                    v.wait_ge(wbm, e + 1)
                    v.tensor_tensor(h_sb[0:16, e * T:(e + 1) * T], h_sb[0:16, e * T:(e + 1) * T],
                                    ps_wb[0:16, :], ALU.mult)
                    ev[f"hwd_{e}"] = v.n

            @block.tensor
            def _(t):
                cx.wait_bar(t)
                t.wait_ge(vss, ev["wtev"])
                t.wait_ge(vss, ev[f"cev_{NGRP - 1}"])
                for e in range(E):
                    if e > 0:
                        t.wait_ge(vss, ev[f"hwd_{e - 1}"])
                    ins = None
                    for q4 in range(4):
                        ins = t.matmul(ps_wb[0:16, q4 * 512:(q4 + 1) * 512],
                                       esel_sb[0:6, e * 16:(e + 1) * 16],
                                       wt_sb[0:6, q4 * 512:(q4 + 1) * 512],
                                       start=True, stop=True)
                    ins.then_inc(wbm, 1)

        with cx.psum("psy0", [128, 512], F32) as psy0, \
                nc.psum_tensor("psy1", [128, 512], F32) as psy1:
            psy = [psy0, psy1]

            @block.vector
            def _(v):
                v = cx.V(v)
                nn = 0
                for ft in range(CH):
                    dw(v, cld, 2 * (ft + 1))
                    if ft >= 1:
                        dw(v, st, ft)
                    for q4 in range(4):
                        v.wait_ge(ymm, nn + 1)
                        tc = slice(q4 * 512, (q4 + 1) * 512)
                        osl = o_sb[:, tc]
                        v.tensor_scalar(osl, psy[nn % 2][:, :], bb_sb[:, ft:ft + 1],
                                        None, ALU.add)
                        ev[f"yev_{nn}"] = v.n
                        v.tensor_add(osl, osl, ac[:, tc])
                        v.tensor_add(osl, osl, xc[:, tc])
                        nn += 1
                    ev[f"od_{ft}"] = v.n

            @block.tensor
            def _(t):
                cx.wait_bar(t)
                t.wait_ge(vss, ev[f"hwd_{E - 1}"])
                nn = 0
                for ft in range(CH):
                    dw(t, wld, (ft + 1))
                    for q4 in range(4):
                        if nn >= 2:
                            t.wait_ge(vss, ev[f"yev_{nn - 2}"])
                        p = psy[nn % 2]
                        tc = slice(q4 * 512, (q4 + 1) * 512)
                        ins = None
                        for c in range(CH):
                            t.matmul(p[:, :], wbb[ft % 2][:, c * 128:(c + 1) * 128],
                                     xn_sb[:, c * T + q4 * 512: c * T + (q4 + 1) * 512],
                                     start=(c == 0), stop=False)
                        t.matmul(p[:, :], bs_sb[0:16, ft * 128:(ft + 1) * 128],
                                 tmp_sb[0:16, tc], start=False, stop=False)
                        for e in range(E):
                            ins = t.matmul(p[:, :], br_sb[0:16, e * D + ft * 128: e * D + (ft + 1) * 128],
                                           h_sb[0:16, e * T + q4 * 512: e * T + (q4 + 1) * 512],
                                           start=False, stop=(e == E - 1))
                        ins.then_inc(ymm, 1)
                        nn += 1

            @block.gpsimd
            def _(g):
                g = cx.G(g)
                cx.wait_bar(g)
                g.dma_start(out=idf[:, :], in_=identf).then_inc(ld, 16)
                for c in range(CH):
                    g.dma_start(out=xn_sb[:, c * T:(c + 1) * T],
                                in_=xnT[c * 128:(c + 1) * 128, :]).then_inc(ld, 16)
                g.dma_start(out=bb_sb[:, :], in_=bbaseT).then_inc(ld, 16)
                g.dma_start(out=as_sb[:, :].rearrange("p (c r) -> p c r", r=R),
                            in_=a_s.rearrange("(c p) r -> p c r", p=128)).then_inc(ld, 16)
                g.dma_start(out=bs_sb[0:16, :], in_=b_s).then_inc(ld, 16)
                for e in range(E):
                    g.dma_start(out=ar_sb[:, e * CH * R:(e + 1) * CH * R].rearrange(
                                    "p (c r) -> p c r", r=R),
                                in_=a_r[e, :, :].rearrange("(c p) r -> p c r", p=128)).then_inc(ld, 16)
                for e in range(E):
                    g.dma_start(out=br_sb[0:16, e * D:(e + 1) * D], in_=b_r[e, :, :]).then_inc(ld, 16)
                g.dma_start(out=wr_sb[:, :].rearrange("p (c e) -> p c e", e=E),
                            in_=wrout.rearrange("(c p) e -> p c e", p=128)).then_inc(ld, 16)
                g.dma_start(out=bro_b[:, :], in_=brout.broadcast_to([128, E])).then_inc(ld, 16)
                g.dma_start(out=esel_sb[0:6, :], in_=esel).then_inc(ld, 16)
                for ft in range(CH):
                    if ft >= 2:
                        g.wait_ge(ymm, 4 * (ft - 1))
                    g.dma_start(out=wbb[ft % 2][:, :].rearrange("p (c f) -> p c f", f=128),
                                in_=wbase[:, ft * 128:(ft + 1) * 128].rearrange(
                                    "(c p) f -> p c f", p=128)).then_inc(wld, 16)
                    if ft > 0:
                        g.wait_ge(vss, ev[f"od_{ft - 1}"])
                    g.dma_start(out=xc[:, :], in_=xT[ft * 128:(ft + 1) * 128, :]).then_inc(cld, 16)
                    g.dma_start(out=ac[:, :], in_=aT[ft * 128:(ft + 1) * 128, :]).then_inc(cld, 16)
                    g.wait_ge(vss, ev[f"od_{ft}"])
                    g.dma_start(out=outT[ft * 128:(ft + 1) * 128, :],
                                in_=o_sb[:, :]).then_inc(st, 16)
    cx.end_phase([(ld, 35), (wld, 16), (cld, 32), (rta, 1), (trd, 1),
                  (cmm, 28), (wbm, 6), (ymm, 64), (st, 1)])



def emit_ag(cx, ins_outs):
    nc = cx.nc
    cc = cx.sem("cc")
    block = cx.block

    @block.gpsimd
    def _(g):
        g = cx.G(g)
        cx.wait_bar(g)
        for (src_t, dst_t) in ins_outs:
            g.collective_compute(
                "AllGather", ALU.bypass,
                replica_groups=[[0, 1], [2, 3], [4, 5], [6, 7]],
                ins=[src_t.ap()], outs=[dst_t.ap()],
            ).then_inc(cc, 1)
        g.wait_ge(cc, len(ins_outs))
    cx.end_phase([(cc, len(ins_outs))])


def build_fused():
    nc = _nc()
    ap = lambda h: h.ap()
    xT = nc.dram_tensor("xT", [D, T], F32, kind="ExternalInput").ap()
    wproj = nc.dram_tensor("wproj", [D, 3 * D], F32, kind="ExternalInput").ap()
    wout = nc.dram_tensor("wout", [D, D], F32, kind="ExternalInput").ap()
    w1T = nc.dram_tensor("w1T", [D, HID], F32, kind="ExternalInput").ap()
    w2T = nc.dram_tensor("w2T", [D, HID], F32, kind="ExternalInput").ap()
    w3T = nc.dram_tensor("w3T", [HID, D], F32, kind="ExternalInput").ap()
    wqkv1 = nc.dram_tensor("wqkv1", [D, 3 * D], F32, kind="ExternalInput").ap()
    wo1 = nc.dram_tensor("wo1", [D, D], F32, kind="ExternalInput").ap()
    wbase = nc.dram_tensor("wbase", [D, D], F32, kind="ExternalInput").ap()
    bbaseT = nc.dram_tensor("bbaseT", [128, 16], F32, kind="ExternalInput").ap()
    a_s = nc.dram_tensor("a_s", [D, R], F32, kind="ExternalInput").ap()
    b_s = nc.dram_tensor("b_s", [R, D], F32, kind="ExternalInput").ap()
    a_r = nc.dram_tensor("a_r", [E, D, R], F32, kind="ExternalInput").ap()
    b_r = nc.dram_tensor("b_r", [E, R, D], F32, kind="ExternalInput").ap()
    wrout = nc.dram_tensor("wrout", [D, E], F32, kind="ExternalInput").ap()
    brout = nc.dram_tensor("brout", [1, E], F32, kind="ExternalInput").ap()
    identb = nc.dram_tensor("identb", [128, 128], BF16, kind="ExternalInput").ap()
    identf = nc.dram_tensor("identf", [128, 128], F32, kind="ExternalInput").ap()
    esel = nc.dram_tensor("esel", [E, E * 16], F32, kind="ExternalInput").ap()
    maskab = nc.dram_tensor("maskab", [256, 128], F32, kind="ExternalInput").ap()
    maskabT = nc.dram_tensor("maskabT", [256, 128], F32, kind="ExternalInput").ap()
    outT = nc.dram_tensor("outT", [D, T], F32, kind="ExternalOutput").ap()

    thr = nc.dram_tensor("thr_i", [1, 8], F32, kind="Internal")
    xn = nc.dram_tensor("xn_i", [D, T], BF16, kind="Internal")
    q_t = nc.dram_tensor("q_i", [T, D], BF16, kind="Internal")
    k_own = nc.dram_tensor("k_i", [D, T], BF16, kind="Internal")
    v_own = nc.dram_tensor("v_i", [T, D], BF16, kind="Internal")
    gk = nc.dram_tensor("gk_i", [2 * D, T], BF16, kind="Internal")
    gv = nc.dram_tensor("gv_i", [2 * T, D], BF16, kind="Internal")
    att = nc.dram_tensor("att_i", [T, D], BF16, kind="Internal")
    aTb = nc.dram_tensor("aT_i", [D, T], BF16, kind="Internal")
    hid = nc.dram_tensor("hid_i", [HID, T], BF16, kind="Internal")
    x1 = nc.dram_tensor("x1_i", [D, T], F32, kind="Internal")
    pt = nc.dram_tensor("pt_i", [T, 4096], BF16, kind="Internal")

    es = ExitStack()
    with es:
        block = es.enter_context(nc.Block())
        cx = FCtx(nc, es, block)
        emit_gamma(cx, w1T, w2T, w3T, thr.ap())
        emit_ln(cx, xT, xn.ap())
        emit_qkv(cx, xn.ap(), wproj, q_t.ap(), k_own.ap(), v_own.ap())
        emit_ag(cx, [(k_own, gk), (v_own, gv)])
        emit_attn0(cx, q_t.ap(), gk.ap(), gv.ap(), maskabT, identb, att.ap())
        emit_oproj(cx, att.ap(), wout, aTb.ap())
        emit_ffna(cx, xn.ap(), w1T, w2T, thr.ap(), hid.ap())
        emit_ffnb(cx, hid.ap(), w3T, thr.ap(), xT, aTb.ap(), x1.ap())
        emit_ln(cx, x1.ap(), xn.ap())
        emit_qkv(cx, xn.ap(), wqkv1, q_t.ap(), k_own.ap(), v_own.ap())
        emit_ag(cx, [(k_own, gk), (v_own, gv)])
        emit_attn1b1(cx, q_t.ap(), gk.ap(), maskab, identb, pt.ap())
        emit_attn1b2(cx, pt.ap(), gv.ap(), identb, att.ap())
        emit_oproj(cx, att.ap(), wo1, aTb.ap())
        emit_ffn1(cx, xn.ap(), wbase, bbaseT, a_s, b_s, a_r, b_r, wrout, brout,
                  identf, esel, x1.ap(), aTb.ap(), outT)
    return nc


_NCS = {}


def _get(name, builder):
    if name not in _NCS:
        _NCS[name] = builder()
    return _NCS[name]


def _f32(a):
    return np.ascontiguousarray(a, dtype=np.float32)


def kernel(L0_attn_x, L0_attn_w_project, L0_attn_w_out, L0_ffn_w1, L0_ffn_w2, L0_ffn_w3,
           L1_attn_W_q, L1_attn_W_k, L1_attn_W_v, L1_attn_W_o,
           L1_ffn_W_base, L1_ffn_b_base, L1_ffn_A_s, L1_ffn_B_s, L1_ffn_A_r, L1_ffn_B_r,
           L1_ffn_W_router, L1_ffn_b_router):
    x = np.asarray(L0_attn_x, dtype=np.float32)
    B = x.shape[0]

    xTs = []
    for c in range(8):
        seq, par = c // 2, c % 2
        rows = x[seq].reshape(32, 128, D)[par::2].reshape(T, D)
        xTs.append(_f32(rows.T))

    identb = np.eye(128, dtype=ml_dtypes.bfloat16)
    identf = np.eye(128, dtype=np.float32)
    tri = np.where(np.arange(128)[None, :] <= np.arange(128)[:, None], 0.0, -1e9).astype(np.float32)
    full = np.full((128, 128), -1e9, np.float32)
    zero = np.zeros((128, 128), np.float32)
    maskab = []
    for c in range(8):
        par = c % 2
        mA = tri if par == 0 else zero
        mB = full if par == 0 else tri
        maskab.append(np.ascontiguousarray(np.concatenate([mA, mB], axis=0)))
    maskabT = [np.ascontiguousarray(np.concatenate([m[0:128].T, m[128:256].T], axis=0))
               for m in maskab]

    wproj = _f32(np.asarray(L0_attn_w_project, np.float32).T)
    wout = _f32(np.asarray(L0_attn_w_out, np.float32).T)
    w1T = _f32(np.asarray(L0_ffn_w1, np.float32).T)
    w2T = _f32(np.asarray(L0_ffn_w2, np.float32).T)
    w3T = _f32(np.asarray(L0_ffn_w3, np.float32).T)
    wqkv1 = _f32(np.concatenate([np.asarray(L1_attn_W_q, np.float32),
                                 np.asarray(L1_attn_W_k, np.float32),
                                 np.asarray(L1_attn_W_v, np.float32)], axis=1))
    wo1 = _f32(np.asarray(L1_attn_W_o, np.float32))
    wbase = _f32(np.asarray(L1_ffn_W_base, np.float32))
    bbaseT = _f32(np.asarray(L1_ffn_b_base, np.float32).reshape(16, 128).T)
    a_s = _f32(np.asarray(L1_ffn_A_s, np.float32))
    b_s = _f32(np.asarray(L1_ffn_B_s, np.float32))
    a_r = _f32(np.asarray(L1_ffn_A_r, np.float32))
    b_r = _f32(np.asarray(L1_ffn_B_r, np.float32))
    wrout = _f32(np.asarray(L1_ffn_W_router, np.float32))
    brout = _f32(np.asarray(L1_ffn_b_router, np.float32).reshape(1, E))
    esel = np.zeros((E, E * 16), np.float32)
    for e in range(E):
        esel[e, e * 16:(e + 1) * 16] = 1.0

    nc = _get("fused", build_fused)
    in_maps = []
    for c in range(8):
        in_maps.append({
            "xT": xTs[c], "wproj": wproj, "wout": wout, "w1T": w1T, "w2T": w2T,
            "w3T": w3T, "wqkv1": wqkv1, "wo1": wo1, "wbase": wbase, "bbaseT": bbaseT,
            "a_s": a_s, "b_s": b_s, "a_r": a_r, "b_r": b_r, "wrout": wrout,
            "brout": brout, "identb": identb, "identf": identf, "esel": esel,
            "maskab": maskab[c], "maskabT": maskabT[c],
        })
    res = run_bass_kernel_spmd(nc, in_maps, core_ids=list(range(8))).results

    out = np.empty((B, 4096, D), np.float32)
    for c in range(8):
        seq, par = c // 2, c % 2
        rows = np.asarray(res[c]["outT"], np.float32).T
        out[seq].reshape(32, 128, D)[par::2] = rows.reshape(NT, 128, D)
    return out


# version 11 (final)
# speedup vs baseline: 1.5898x; 1.0072x over previous
import numpy as np
import ml_dtypes
from contextlib import ExitStack

import concourse.bass as bass
import concourse.mybir as mybir
from concourse.bass_utils import run_bass_kernel_spmd

F32 = mybir.dt.float32
BF16 = mybir.dt.bfloat16
AF = mybir.ActivationFunctionType
ALU = mybir.AluOpType
AX = mybir.AxisListType

D = 2048
T = 2048
NT = 16
CH = 16
H = 16
HID = 5632
NHT = 44
E = 6
R = 16
SCALE = float(1.0 / np.sqrt(128.0))
LN_EPS = 1e-5
Q_EPS = 1e-8

EXT = [j + 1 for j in range(NT)]


def _chunks(w):
    out = []
    off = 0
    while off < w:
        cw = min(512, w - off)
        out.append((off, cw))
        off += cw
    return out


class _DPair:

    def __init__(self, sems):
        self.sems = sems
        self.w = len(sems)


def dw(eng, s, n):
    if isinstance(s, _DPair):
        for r in range(s.w):
            cnt = (n - r + s.w - 1) // s.w
            if cnt > 0:
                eng.wait_ge(s.sems[r], 16 * cnt)
    else:
        eng.wait_ge(s, 16 * n)


class _SerialQ:

    def __init__(self, eng):
        self._e = eng
        self._counts = {}

    def dma_start(self, out, in_, **kw):
        ins = self._e.dma_start(out=out, in_=in_, **kw)
        return _PendInc(self, ins)

    def __getattr__(self, k):
        return getattr(self._e, k)


class _PendInc:
    def __init__(self, q, ins):
        self._q = q
        self._ins = ins

    def then_inc(self, sem, k):
        q = self._q
        if isinstance(sem, _DPair):
            i = q._counts.get(id(sem), 0)
            self._ins.then_inc(sem.sems[i % sem.w], 16)
            q._counts[id(sem)] = i + 1
            nxt = i + 1
            if nxt >= sem.w:
                q._e.wait_ge(sem.sems[nxt % sem.w], 16 * (nxt // sem.w))
            return self._ins
        self._ins.then_inc(sem, k)
        key = id(sem)
        q._counts[key] = q._counts.get(key, 0) + k
        q._e.wait_ge(sem, q._counts[key])
        return self._ins


_SER_OPS = {
    "tensor_tensor", "tensor_add", "tensor_sub", "tensor_mul", "tensor_max",
    "tensor_scalar", "tensor_single_scalar", "tensor_scalar_add", "tensor_scalar_sub",
    "tensor_scalar_mul", "tensor_scalar_max", "tensor_scalar_min", "tensor_relu",
    "tensor_reduce", "reduce_max", "reduce_sum", "scalar_tensor_tensor",
    "tensor_tensor_scan", "tensor_copy", "memset", "memzero", "reciprocal",
    "select", "activation", "sign", "square", "sqrt", "mul", "copy",
    "add", "bn_stats", "bn_aggr", "pool", "pool_avg", "pool_max",
}


class _SerialEng:

    def __init__(self, eng, sem):
        self._e = eng
        self._s = sem
        self.n = 0

    def __getattr__(self, k):
        a = getattr(self._e, k)
        if k not in _SER_OPS:
            return a

        def wrap(*args, **kw):
            if self.n > 0:
                self._e.wait_ge(self._s, self.n)
            ins = a(*args, **kw)
            ins.then_inc(self._s, 1)
            self.n += 1
            return ins
        return wrap




class FCtx:

    def __init__(self, nc, es, block):
        self.nc = nc
        self.es = es
        self.block = block
        self.vss = es.enter_context(nc.semaphore("vss"))
        self.sss = es.enter_context(nc.semaphore("sss"))
        self.vw = None
        self.sw = None
        self.gq = _SerialQ(None)
        self.syq = _SerialQ(None)
        self._nsem = 0
        self._keep = []
        self.bar = {"events": [], "vss": 0, "sss": 0}

    def sem(self, name):
        self._nsem += 1
        return self.es.enter_context(self.nc.semaphore(f"{name}{self._nsem}"))

    def dsem(self, name, w=2):
        p = _DPair([self.sem(name + chr(97 + i)) for i in range(w)])
        self._keep.append(p)
        return p

    def sbuf(self, name, shape, dt, **kw):
        self._nsem += 1
        return self.nc.sbuf_tensor(f"{name}_{self._nsem}", shape, dt, **kw)

    def psum(self, name, shape, dt=F32, **kw):
        self._nsem += 1
        return self.nc.psum_tensor(f"{name}_{self._nsem}", shape, dt, **kw)

    def V(self, raw):
        if self.vw is None:
            self.vw = _SerialEng(raw, self.vss)
        else:
            self.vw._e = raw
        return self.vw

    def S(self, raw):
        if self.sw is None:
            self.sw = _SerialEng(raw, self.sss)
        else:
            self.sw._e = raw
        return self.sw

    def G(self, raw):
        self.gq._e = raw
        return self.gq

    def SY(self, raw):
        self.syq._e = raw
        return self.syq

    def sw_n(self):
        return self.sw.n if self.sw is not None else 0

    def vw_n(self):
        return self.vw.n if self.vw is not None else 0

    def wait_bar(self, eng):
        for (sem, val) in self.bar["events"]:
            if isinstance(sem, _DPair):
                dw(eng, sem, val)
            else:
                eng.wait_ge(sem, val)
        if self.bar["vss"] > 0:
            eng.wait_ge(self.vss, self.bar["vss"])
        if self.bar["sss"] > 0:
            eng.wait_ge(self.sss, self.bar["sss"])

    def end_phase(self, events):
        self.bar = {"events": list(events), "vss": self.vw_n(), "sss": self.sw_n()}

def _nc():
    return bass.Bass("TRN2", target_bir_lowering=False, debug=False)


def emit_ln(cx, xT, xnT):
    nc = cx.nc
    es = ExitStack()
    with es:
        x_sb = es.enter_context(cx.sbuf("x_sb", [128, CH * T], F32))
        xn_sb = es.enter_context(cx.sbuf("xn_sb", [128, CH * T], BF16))
        x2_sb = es.enter_context(cx.sbuf("x2_sb", [128, 512], F32))
        ones_c = es.enter_context(cx.sbuf("ones_c", [128, 1], F32))
        ones_r = es.enter_context(cx.sbuf("ones_r", [1, 128], F32))
        epsv = es.enter_context(cx.sbuf("epsv", [1, 1], F32))
        mean_sb = es.enter_context(cx.sbuf("mean_sb", [1, 512], F32))
        var_sb = es.enter_context(cx.sbuf("var_sb", [1, 512], F32))
        rstd_sb = es.enter_context(cx.sbuf("rstd_sb", [1, 512], F32))
        tmp_sb = es.enter_context(cx.sbuf("tmp_sb", [128, 512], F32))
        ps_s = es.enter_context(cx.psum("ps_s", [1, 512], F32))
        ps_s2 = es.enter_context(cx.psum("ps_s2", [1, 512], F32))
        ps_bm = es.enter_context(cx.psum("ps_bm", [128, 512], F32))
        ps_br = es.enter_context(cx.psum("ps_br", [128, 512], F32))
        ld = cx.dsem("ld")
        mmd = cx.sem("mmd")
        pass
        pass
        st = cx.dsem("st")
        block = cx.block
        vss = cx.vss
        sss = cx.sss
        SS0 = cx.sw_n()
        ev = {}

        @block.vector
        def _(v):
            v = cx.V(v)
            cx.wait_bar(v)
            v.memset(ones_c[:, :], 1.0)
            v.memset(epsv[0:1, :], LN_EPS)
            v.memset(ones_r[:, :], 1.0)
            ev["ones"] = v.n
            for q4 in range(4):
                if q4 > 0:
                    v.wait_ge(sss, SS0 + 2 * q4)
                for c in range(CH):
                    dw(v, ld, (c + 1))
                    if q4 * CH + c > 0:
                        v.wait_ge(mmd, q4 * 17 + c)
                    src = x_sb[:, c * T + q4 * 512: c * T + (q4 + 1) * 512]
                    v.tensor_mul(x2_sb[:, :], src, src)
                    ev[f"x2_{q4}_{c}"] = v.n
                v.wait_ge(mmd, q4 * 17 + 16)
                v.tensor_scalar_mul(mean_sb[0:1, :], ps_s[0:1, :], 1.0 / D)
                v.tensor_scalar_mul(var_sb[0:1, :], ps_s2[0:1, :], 1.0 / D)
                v.tensor_mul(tmp_sb[0:1, :], mean_sb[0:1, :], mean_sb[0:1, :])
                v.tensor_sub(var_sb[0:1, :], var_sb[0:1, :], tmp_sb[0:1, :])
                ev[f"stats_{q4}"] = v.n
                v.wait_ge(mmd, 17 * (q4 + 1))
                for c in range(CH):
                    src = x_sb[:, c * T + q4 * 512: c * T + (q4 + 1) * 512]
                    v.tensor_sub(tmp_sb[:, :], src, ps_bm[:, :])
                    v.tensor_mul(xn_sb[:, c * T + q4 * 512: c * T + (q4 + 1) * 512],
                                 tmp_sb[:, :], ps_br[:, :])
                ev[f"xn_{q4}"] = v.n
            ev["xn_all"] = v.n

        @block.scalar
        def _(s):
            s = cx.S(s)
            cx.wait_bar(s)
            for q4 in range(4):
                s.wait_ge(vss, ev[f"stats_{q4}"])
                s.activation(rstd_sb[0:1, :], var_sb[0:1, :], AF.Ln, bias=epsv[0:1, 0:1])
                s.activation(rstd_sb[0:1, :], rstd_sb[0:1, :], AF.Exp, scale=-0.5)

        @block.tensor
        def _(t):
            cx.wait_bar(t)
            t.wait_ge(vss, ev["ones"])
            for q4 in range(4):
                if q4 > 0:
                    t.wait_ge(vss, ev[f"stats_{q4 - 1}"])
                    t.wait_ge(sss, SS0 + 2 * q4)
                for c in range(CH):
                    dw(t, ld, (c + 1))
                    t.wait_ge(vss, ev[f"x2_{q4}_{c}"])
                    t.matmul(ps_s[0:1, :], ones_c[:, 0:1],
                             x_sb[:, c * T + q4 * 512: c * T + (q4 + 1) * 512],
                             start=(c == 0), stop=(c == CH - 1))
                    t.matmul(ps_s2[0:1, :], ones_c[:, 0:1], x2_sb[:, :],
                             start=(c == 0), stop=(c == CH - 1)).then_inc(mmd, 1)
                t.wait_ge(sss, SS0 + 2 * (q4 + 1))
                t.wait_ge(vss, ev[f"stats_{q4}"])
                if q4 > 0:
                    t.wait_ge(vss, ev[f"xn_{q4 - 1}"])
                t.matmul(ps_bm[:, :], ones_r[0:1, :], mean_sb[0:1, :], start=True, stop=True)
                t.matmul(ps_br[:, :], ones_r[0:1, :], rstd_sb[0:1, :],
                         start=True, stop=True).then_inc(mmd, 1)

        @block.gpsimd
        def _(g):
            g = cx.G(g)
            cx.wait_bar(g)
            for c in range(CH):
                g.dma_start(out=x_sb[:, c * T:(c + 1) * T],
                            in_=xT[c * 128:(c + 1) * 128, :]).then_inc(ld, 16)
            g.wait_ge(vss, ev["xn_all"])
            for c in range(CH):
                g.dma_start(out=xnT[c * 128:(c + 1) * 128, :],
                            in_=xn_sb[:, c * T:(c + 1) * T]).then_inc(st, 16)
    cx.end_phase([(ld, 16), (mmd, 68), (st, 16)])



def emit_qkv(cx, xnT, w, q_tok, k_sw, v_tok):
    jobs = [(0, q_tok, "tok"), (D, k_sw, "feat"), (2 * D, v_tok, "tok")]
    jobs = [(0, q_tok, "tok"), (D, k_sw, "feat"), (2 * D, v_tok, "tok")]
    nc = cx.nc
    es = ExitStack()
    with es:
        xn_sb = es.enter_context(cx.sbuf("xn_sb", [128, CH * T], BF16))
        w_sb = es.enter_context(cx.sbuf("w_sb", [128, CH * D], BF16))
        stage = es.enter_context(cx.sbuf("stage", [128, 2 * D], BF16))
        ps = [es.enter_context(cx.psum(f"ps{i}", [128, D], F32)) for i in range(2)]
        ld = cx.dsem("ld")
        mmg = cx.sem("mmg")
        pass
        st = cx.dsem("st")
        block = cx.block
        vss = cx.vss
        sss = cx.sss
        SS0 = cx.sw_n()

        @block.scalar
        def _(s):
            s = cx.S(s)
            cx.wait_bar(s)
            for n in range(48):
                s.wait_ge(mmg, n + 1)
                if n >= 2:
                    dw(s, st, (n - 1))
                s.activation(stage[:, (n % 2) * D:(n % 2 + 1) * D],
                             ps[n % 2][:, :], AF.Copy)

        @block.tensor
        def _(t):
            cx.wait_bar(t)
            dw(t, ld, 16)
            for ji, (cb, out, mode) in enumerate(jobs):
                dw(t, ld, 16 + (ji + 1) * 16)
                for ot in range(16):
                    n = ji * 16 + ot
                    if n >= 2:
                        t.wait_ge(sss, SS0 + n - 1)
                    p = ps[n % 2]
                    ins = None
                    for c in range(CH):
                        last = (c == CH - 1)
                        for q4 in range(4):
                            if mode == "feat":
                                lhsT = w_sb[:, c * D + ot * 128: c * D + (ot + 1) * 128]
                                rhs = xn_sb[:, c * T + q4 * 512: c * T + (q4 + 1) * 512]
                            else:
                                lhsT = xn_sb[:, c * T + ot * 128: c * T + (ot + 1) * 128]
                                rhs = w_sb[:, c * D + q4 * 512: c * D + (q4 + 1) * 512]
                            ins = t.matmul(p[:, q4 * 512:(q4 + 1) * 512], lhsT, rhs,
                                           start=(c == 0), stop=last)
                    ins.then_inc(mmg, 1)

        @block.gpsimd
        def _(g):
            g = cx.G(g)
            cx.wait_bar(g)
            for c in range(CH):
                g.dma_start(out=xn_sb[:, c * T:(c + 1) * T],
                            in_=xnT[c * 128:(c + 1) * 128, :]).then_inc(ld, 16)
            for ji, (cb, out, mode) in enumerate(jobs):
                if ji > 0:
                    g.wait_ge(mmg, ji * 16)
                for c in range(CH):
                    g.dma_start(out=w_sb[:, c * D:(c + 1) * D],
                                in_=w[c * 128:(c + 1) * 128, cb:cb + D]).then_inc(ld, 16)
                for ot in range(16):
                    g.wait_ge(sss, SS0 + ji * 16 + ot + 1)
                    g.dma_start(out=out[ot * 128:(ot + 1) * 128, :],
                                in_=stage[:, (ot % 2) * D:(ot % 2 + 1) * D]).then_inc(st, 16)
    cx.end_phase([(ld, 64), (mmg, 48), (st, 48)])



def emit_gamma(cx, w1T, w2T, w3T, thr):
    mats = [(w1T, CH, HID), (w2T, CH, HID), (w3T, NHT, D)]
    nslots = CH + CH + NHT
    mats = [(w1T, CH, HID), (w2T, CH, HID), (w3T, NHT, D)]
    nslots = CH + CH + NHT
    nc = cx.nc
    es = ExitStack()
    with es:
        wbuf = [es.enter_context(cx.sbuf(f"wbuf{i}", [128, HID], F32)) for i in range(2)]
        acc = es.enter_context(cx.sbuf("acc", [128, nslots], F32))
        ones_c = es.enter_context(cx.sbuf("ones_c", [128, 1], F32))
        tsb = es.enter_context(cx.sbuf("tsb", [1, 8], F32))
        ps = es.enter_context(cx.psum("ps", [1, nslots], F32))
        ld = cx.dsem("ld")
        tfin = cx.sem("tfin")
        pass
        st = cx.dsem("st")
        block = cx.block
        vss = cx.vss
        sss = cx.sss
        SS0 = cx.sw_n()
        ev = {}

        @block.vector
        def _(v):
            v = cx.V(v)
            cx.wait_bar(v)
            v.memset(ones_c[:, :], 1.0)
            v.memset(tsb[0:1, :], 0.0)
            i = 0
            for (w, nch, cols) in mats:
                for c in range(nch):
                    dw(v, ld, (i + 1))
                    v.tensor_reduce(acc[:, i:i + 1], wbuf[i % 2][:, :cols], AX.X, ALU.add,
                                    apply_absolute_value=True)
                    ev[f"rd_{i}"] = v.n
                    i += 1
            v.wait_ge(tfin, 1)
            base = 0
            for mi, (w, nch, cols) in enumerate(mats):
                nel = float(nch) * 128.0 * float(cols)
                v.tensor_reduce(tsb[0:1, mi:mi + 1], ps[0:1, base:base + nch], AX.X, ALU.add)
                v.tensor_scalar(tsb[0:1, mi:mi + 1], tsb[0:1, mi:mi + 1], 1.0 / nel, Q_EPS,
                                ALU.mult, ALU.add)
                v.tensor_scalar_mul(tsb[0:1, mi:mi + 1], tsb[0:1, mi:mi + 1], 0.5)
                base += nch
            for mi in range(3):
                v.tensor_scalar_mul(tsb[0:1, 4 + mi:5 + mi], tsb[0:1, mi:mi + 1], -1.0)
            ev["fin"] = v.n

        @block.tensor
        def _(t):
            cx.wait_bar(t)
            t.wait_ge(vss, ev[f"rd_{nslots - 1}"])
            t.matmul(ps[0:1, :], ones_c[:, 0:1], acc[:, :], start=True, stop=True).then_inc(tfin, 1)

        @block.gpsimd
        def _(g):
            g = cx.G(g)
            cx.wait_bar(g)
            i = 0
            for (w, nch, cols) in mats:
                for c in range(nch):
                    if i >= 2:
                        g.wait_ge(vss, ev[f"rd_{i - 2}"])
                    g.dma_start(out=wbuf[i % 2][:, :cols],
                                in_=w[c * 128:(c + 1) * 128, :]).then_inc(ld, 16)
                    i += 1
            g.wait_ge(vss, ev["fin"])
            g.dma_start(out=thr, in_=tsb[0:1, :]).then_inc(st, 16)
    cx.end_phase([(ld, 76), (tfin, 1), (st, 1)])



def emit_ffna(cx, xnT, w1T, w2T, thr, hid):
    nc = cx.nc
    es = ExitStack()
    with es:
        xn_sb = es.enter_context(cx.sbuf("xn_sb", [128, CH * T], BF16))
        thr_b = es.enter_context(cx.sbuf("thr_b", [128, 8], F32))
        w1b = [es.enter_context(cx.sbuf(f"w1b{i}", [128, D], F32)) for i in range(2)]
        w2b = [es.enter_context(cx.sbuf(f"w2b{i}", [128, D], F32)) for i in range(2)]
        q1 = [es.enter_context(cx.sbuf(f"q1{i}", [128, D], BF16)) for i in range(2)]
        q2 = [es.enter_context(cx.sbuf(f"q2{i}", [128, D], BF16)) for i in range(2)]
        qa = es.enter_context(cx.sbuf("qa", [128, D], BF16))
        qb = es.enter_context(cx.sbuf("qb", [128, D], BF16))
        sig = es.enter_context(cx.sbuf("sig", [128, T], BF16))
        hsb = [es.enter_context(cx.sbuf(f"hsb{i}", [128, T], BF16)) for i in range(2)]
        ld = cx.dsem("ld")
        mmd = cx.sem("mmd")
        pass
        pass
        st = cx.dsem("st")
        ps1 = es.enter_context(cx.psum("ps1", [128, T], F32))
        ps2 = es.enter_context(cx.psum("ps2", [128, T], F32))
        block = cx.block
        vss = cx.vss
        sss = cx.sss
        SS0 = cx.sw_n()
        ev = {}

        @block.vector
        def _(v):
            v = cx.V(v)
            cx.wait_bar(v)
            for ht in range(NHT):
                dw(v, ld, (17 + 2 * (ht + 1)))
                if ht >= 2:
                    v.wait_ge(mmd, ht - 1)
                v.tensor_scalar(qa[:, :], w1b[ht % 2][:, :], thr_b[:, 0:1], None, ALU.is_gt)
                v.tensor_scalar(qb[:, :], w1b[ht % 2][:, :], thr_b[:, 4:5], None, ALU.is_lt)
                v.tensor_sub(q1[ht % 2][:, :], qa[:, :], qb[:, :])
                v.tensor_scalar(qa[:, :], w2b[ht % 2][:, :], thr_b[:, 1:2], None, ALU.is_gt)
                v.tensor_scalar(qb[:, :], w2b[ht % 2][:, :], thr_b[:, 5:6], None, ALU.is_lt)
                v.tensor_sub(q2[ht % 2][:, :], qa[:, :], qb[:, :])
                ev[f"qd_{ht}"] = v.n
                v.wait_ge(sss, SS0 + ht + 1)
                if ht >= 2:
                    dw(v, st, (ht - 1))
                v.tensor_tensor(hsb[ht % 2][:, :], sig[:, :], ps2[:, :], ALU.mult)
                ev[f"hdn_{ht}"] = v.n

        @block.scalar
        def _(s):
            s = cx.S(s)
            cx.wait_bar(s)
            for ht in range(NHT):
                s.wait_ge(mmd, ht + 1)
                if ht >= 1:
                    s.wait_ge(vss, ev[f"hdn_{ht - 1}"])
                s.activation(sig[:, :], ps1[:, :], AF.Sigmoid)

        @block.tensor
        def _(t):
            cx.wait_bar(t)
            dw(t, ld, 17)
            for ht in range(NHT):
                t.wait_ge(vss, ev[f"qd_{ht}"])
                if ht >= 1:
                    t.wait_ge(sss, SS0 + ht)
                    t.wait_ge(vss, ev[f"hdn_{ht - 1}"])
                ins = None
                for c in range(CH):
                    last = (c == CH - 1)
                    l1 = q1[ht % 2][:, c * 128:(c + 1) * 128]
                    l2 = q2[ht % 2][:, c * 128:(c + 1) * 128]
                    for q4 in range(4):
                        rhs = xn_sb[:, c * T + q4 * 512: c * T + (q4 + 1) * 512]
                        t.matmul(ps1[:, q4 * 512:(q4 + 1) * 512], l1, rhs, start=(c == 0), stop=last)
                    for q4 in range(4):
                        rhs = xn_sb[:, c * T + q4 * 512: c * T + (q4 + 1) * 512]
                        ins = t.matmul(ps2[:, q4 * 512:(q4 + 1) * 512], l2, rhs, start=(c == 0), stop=last)
                ins.then_inc(mmd, 1)

        @block.gpsimd
        def _(g):
            g = cx.G(g)
            cx.wait_bar(g)
            g.dma_start(out=thr_b[:, :], in_=thr.broadcast_to([128, 8])).then_inc(ld, 16)
            for c in range(CH):
                g.dma_start(out=xn_sb[:, c * T:(c + 1) * T],
                            in_=xnT[c * 128:(c + 1) * 128, :]).then_inc(ld, 16)
            for ht in range(NHT):
                if ht >= 2:
                    g.wait_ge(vss, ev[f"qd_{ht - 1}"])
                g.dma_start(out=w1b[ht % 2][:, :].rearrange("p (c h) -> p c h", h=128),
                            in_=w1T[:, ht * 128:(ht + 1) * 128].rearrange("(c p) h -> p c h", p=128)
                            ).then_inc(ld, 16)
                g.dma_start(out=w2b[ht % 2][:, :].rearrange("p (c h) -> p c h", h=128),
                            in_=w2T[:, ht * 128:(ht + 1) * 128].rearrange("(c p) h -> p c h", p=128)
                            ).then_inc(ld, 16)
                g.wait_ge(vss, ev[f"hdn_{ht}"])
                g.dma_start(out=hid[ht * 128:(ht + 1) * 128, :],
                            in_=hsb[ht % 2][:, :]).then_inc(st, 16)
    cx.end_phase([(ld, 105), (mmd, NHT), (st, NHT)])



def emit_ffnb(cx, hid, w3T, thr, xT, aT, x1T):
    NG, GS = 4, 11
    NG, GS = 4, 11
    nc = cx.nc
    es = ExitStack()
    with es:
        y_sb = es.enter_context(cx.sbuf("y_sb", [128, CH * 1024], F32))
        hbuf = es.enter_context(cx.sbuf("hbuf", [128, GS * 1024], BF16))
        wbuf = [es.enter_context(cx.sbuf(f"wbuf{i}", [128, D], F32)) for i in range(2)]
        q3 = es.enter_context(cx.sbuf("q3", [128, GS * D], BF16))
        qa = es.enter_context(cx.sbuf("qa", [128, D], BF16))
        qb = es.enter_context(cx.sbuf("qb", [128, D], BF16))
        thr_b = es.enter_context(cx.sbuf("thr_b", [128, 8], F32))
        xc = es.enter_context(cx.sbuf("xc", [128, 1024], F32))
        ac = es.enter_context(cx.sbuf("ac", [128, 1024], BF16))
        o_sb = es.enter_context(cx.sbuf("o_sb", [128, 1024], F32))
        psy = [es.enter_context(cx.psum(f"psy{i}", [128, 1024], F32)) for i in range(2)]
        ld = cx.dsem("ld")
        cld = cx.dsem("cld")
        mmd = cx.sem("mmd")
        pass
        st = cx.dsem("st")
        block = cx.block
        vss = cx.vss
        sss = cx.sss
        SS0 = cx.sw_n()
        ev = {}

        @block.vector
        def _(v):
            v = cx.V(v)
            cx.wait_bar(v)
            for th in range(2):
                for grp in range(NG):
                    gi = th * NG + grp
                    if gi >= 1:
                        v.wait_ge(mmd, gi * 16)
                    for i in range(GS):
                        dw(v, ld, (1 + gi * 2 * GS + 2 * (i + 1)))
                        v.tensor_scalar(qa[:, :], wbuf[i % 2][:, :], thr_b[:, 2:3], None, ALU.is_gt)
                        v.tensor_scalar(qb[:, :], wbuf[i % 2][:, :], thr_b[:, 6:7], None, ALU.is_lt)
                        v.tensor_sub(q3[:, i * D:(i + 1) * D], qa[:, :], qb[:, :])
                        ev[f"qc_{gi}_{i}"] = v.n
                    ev[f"qg_{gi}"] = v.n
                    for dt in range(16):
                        n = gi * 16 + dt
                        v.wait_ge(mmd, n + 1)
                        dsl = y_sb[:, dt * 1024:(dt + 1) * 1024]
                        if grp == 0:
                            v.tensor_copy(dsl, psy[n % 2][:, :])
                        else:
                            v.tensor_add(dsl, dsl, psy[n % 2][:, :])
                        ev[f"fl_{n}"] = v.n
                for dc in range(CH):
                    n = th * CH + dc
                    dw(v, cld, 2 * (n + 1))
                    if n > 0:
                        dw(v, st, n)
                    dsl = y_sb[:, dc * 1024:(dc + 1) * 1024]
                    v.tensor_add(o_sb[:, :], dsl, ac[:, :])
                    v.tensor_add(o_sb[:, :], o_sb[:, :], xc[:, :])
                    ev[f"od_{n}"] = v.n

        @block.tensor
        def _(t):
            cx.wait_bar(t)
            for th in range(2):
                for grp in range(NG):
                    gi = th * NG + grp
                    t.wait_ge(vss, ev[f"qg_{gi}"])
                    for dt in range(16):
                        n = gi * 16 + dt
                        if n >= 2:
                            t.wait_ge(vss, ev[f"fl_{n - 2}"])
                        p = psy[n % 2]
                        ins = None
                        for i in range(GS):
                            lhsT = q3[:, i * D + dt * 128: i * D + (dt + 1) * 128]
                            t.matmul(p[:, 0:512], lhsT, hbuf[:, i * 1024:i * 1024 + 512],
                                     start=(i == 0), stop=(i == GS - 1))
                            ins = t.matmul(p[:, 512:1024], lhsT, hbuf[:, i * 1024 + 512:(i + 1) * 1024],
                                           start=(i == 0), stop=(i == GS - 1))
                        ins.then_inc(mmd, 1)

        @block.gpsimd
        def _(g):
            g = cx.G(g)
            cx.wait_bar(g)
            g.dma_start(out=thr_b[:, :], in_=thr.broadcast_to([128, 8])).then_inc(ld, 16)
            for th in range(2):
                tcols = slice(th * 1024, (th + 1) * 1024)
                for grp in range(NG):
                    gi = th * NG + grp
                    if gi > 0:
                        g.wait_ge(mmd, gi * 16)
                    for i in range(GS):
                        hc = grp * GS + i
                        g.dma_start(out=hbuf[:, i * 1024:(i + 1) * 1024],
                                    in_=hid[hc * 128:(hc + 1) * 128, tcols]).then_inc(ld, 16)
                        nq = gi * GS + i
                        if nq >= 2:
                            g.wait_ge(vss, ev[f"qc_{(nq - 2) // GS}_{(nq - 2) % GS}"])
                        g.dma_start(out=wbuf[i % 2][:, :],
                                    in_=w3T[hc * 128:(hc + 1) * 128, :]).then_inc(ld, 16)
                for dc in range(CH):
                    n = th * CH + dc
                    if n > 0:
                        g.wait_ge(vss, ev[f"od_{n - 1}"])
                    g.dma_start(out=xc[:, :], in_=xT[dc * 128:(dc + 1) * 128, tcols]).then_inc(cld, 16)
                    g.dma_start(out=ac[:, :], in_=aT[dc * 128:(dc + 1) * 128, tcols]).then_inc(cld, 16)
                    g.wait_ge(vss, ev[f"od_{n}"])
                    g.dma_start(out=x1T[dc * 128:(dc + 1) * 128, tcols], in_=o_sb[:, :]).then_inc(st, 16)
    cx.end_phase([(ld, 177), (cld, 64), (mmd, 128), (st, 32)])



def emit_attn0(cx, q_tok, gk, gv, maskabT, identb, attn_tok):
    nc = cx.nc
    es = ExitStack()
    with es:
        idn = es.enter_context(cx.sbuf("idn", [128, 128], BF16))
        mskT = es.enter_context(cx.sbuf("mskT", [128, 256], F32))
        qbuf = [es.enter_context(cx.sbuf(f"qbuf{i}", [128, D], BF16)) for i in range(2)]
        qt_sb = es.enter_context(cx.sbuf("qt_sb", [128, H * 128], BF16))
        kb = [es.enter_context(cx.sbuf(f"kb{i}", [128, 2 * T], BF16)) for i in range(4)]
        vb = [es.enter_context(cx.sbuf(f"vb{i}", [128, 32 * 129], BF16)) for i in range(4)]
        p_sb = [es.enter_context(cx.sbuf(f"p_sb{i}", [128, 4096], BF16)) for i in range(2)]
        rs = [es.enter_context(cx.sbuf(f"rs{i}", [128, 1], F32)) for i in range(2)]
        ostage = [es.enter_context(cx.sbuf(f"ostage{i}", [128, D], BF16)) for i in range(2)]
        ps_pt = [es.enter_context(cx.psum(f"ps_pt{i}", [128, 1024], BF16)) for i in range(2)]
        ps_s = [es.enter_context(cx.psum(f"ps_s{i}", [128, 512], F32)) for i in range(4)]
        ps_o = [es.enter_context(cx.psum(f"ps_o{i}", [128, 129], F32)) for i in range(2)]
        ld = cx.dsem("ld", w=6)
        trd = cx.sem("trd")
        mmd = cx.sem("mmd")
        avd = cx.sem("avd")
        st = cx.dsem("st")
        block = cx.block
        vss = cx.vss
        sss = cx.sss
        ev = {}

        chunks = []
        for j in range(NT):
            tl = 2 * EXT[j]
            cs = []
            t0 = 0
            while t0 < tl:
                tn = min(4, tl - t0)
                cs.append((t0, tn))
                t0 += tn
            chunks.append(cs)
        nch_t = [len(chunks[j]) for j in range(NT)]
        mm_first = []
        c = 0
        for j in range(NT):
            for h in range(H):
                mm_first.append(c)
                c += nch_t[j]
        nmm_total = c
        diag = []
        for j in range(NT):
            dd = {}
            for par, dtile in ((0, EXT[j] - 1), (1, 2 * EXT[j] - 1)):
                ci = dtile // 4
                dd.setdefault(ci, []).append((dtile % 4, par))
            diag.append(dd)
        ld_q, ld_kv = [], []
        n = 2
        for j in range(NT):
            n += 1
            ld_q.append(n)
            row = []
            for h in range(H):
                n += 3
                row.append(n)
            ld_kv.append(row)

        @block.vector
        def _(v):
            v = cx.V(v)
            cx.wait_bar(v)
            for i in range(4):
                v.memset(vb[i].rearrange("p (t c) -> p t c", c=129)[:, :, 128:129], 1.0)

            def lag_v(pidx):
                pj, phh = divmod(pidx, H)
                v.wait_ge(avd, pidx + 1)
                if phh == 0 and pj >= 2:
                    dw(v, st, (pj - 1))
                v.reciprocal(rs[pidx % 2][:, 0:1], ps_o[pidx % 2][:, 128:129])
                v.tensor_scalar(ostage[pj % 2][:, phh * 128:(phh + 1) * 128],
                                ps_o[pidx % 2][:, 0:128], rs[pidx % 2][:, 0:1],
                                None, ALU.mult)
                ev[f"osd_{pidx}"] = v.n

            for j in range(NT):
                v.wait_ge(trd, 2 * (j + 1))
                v.tensor_copy(qt_sb[:, 0:1024], ps_pt[0][:, :])
                v.tensor_copy(qt_sb[:, 1024:2048], ps_pt[1][:, :])
                ev[f"qtev_{j}"] = v.n
                for h in range(H):
                    idx = j * H + h
                    for ci, tiles in sorted(diag[j].items()):
                        gch = mm_first[idx] + ci
                        v.wait_ge(mmd, gch + 1)
                        for (tloc, par) in tiles:
                            v.tensor_add(ps_s[gch % 4][:, tloc * 128:(tloc + 1) * 128],
                                         ps_s[gch % 4][:, tloc * 128:(tloc + 1) * 128],
                                         mskT[:, par * 128:(par + 1) * 128])
                        ev[f"mskd_{idx}_{ci}"] = v.n
                    if idx >= 1:
                        lag_v(idx - 1)
            lag_v(NT * H - 1)

        @block.scalar
        def _(s):
            s = cx.S(s)
            cx.wait_bar(s)
            nmm = 0
            for j in range(NT):
                for h in range(H):
                    idx = j * H + h
                    for ci, (t0, tn) in enumerate(chunks[j]):
                        s.wait_ge(mmd, nmm + 1)
                        if ci in diag[j]:
                            s.wait_ge(vss, ev[f"mskd_{idx}_{ci}"])
                        if ci == 0 and idx >= 2:
                            s.wait_ge(avd, idx - 1)
                        s.activation(p_sb[idx % 2][:, t0 * 128:(t0 + tn) * 128],
                                     ps_s[nmm % 4][:, 0:tn * 128], AF.Exp, scale=SCALE)
                        ev[f"sev_{nmm}"] = s.n
                        nmm += 1

        @block.tensor
        def _(t):
            cx.wait_bar(t)
            nmm = 0

            def lag_av(pidx):
                pj = pidx // H
                tl = 2 * EXT[pj]
                t.wait_ge(sss, ev[f"sev_{mm_first[pidx] + nch_t[pj] - 1}"])
                if pidx >= 2:
                    t.wait_ge(vss, ev[f"osd_{pidx - 2}"])
                ins = None
                for kt in range(tl):
                    ins = t.matmul(ps_o[pidx % 2][:, 0:129],
                                   p_sb[pidx % 2][:, kt * 128:(kt + 1) * 128],
                                   vb[pidx % 4][:, kt * 129:kt * 129 + 129],
                                   start=(kt == 0), stop=(kt == tl - 1))
                ins.then_inc(avd, 1)

            for j in range(NT):
                dw(t, ld, ld_q[j])
                if j > 0:
                    t.wait_ge(vss, ev[f"qtev_{j - 1}"])
                for half in range(2):
                    ins = None
                    for hh in range(8):
                        h8 = half * 8 + hh
                        ins = t.transpose(ps_pt[half][:, hh * 128:(hh + 1) * 128],
                                          qbuf[j % 2][:, h8 * 128:(h8 + 1) * 128], idn[:, :])
                    ins.then_inc(trd, 1)
                t.wait_ge(vss, ev[f"qtev_{j}"])
                for h in range(H):
                    idx = j * H + h
                    dw(t, ld, ld_kv[j][h])
                    for ci, (t0, tn) in enumerate(chunks[j]):
                        if nmm >= 4:
                            t.wait_ge(sss, ev[f"sev_{nmm - 4}"])
                        ins = None
                        for k in range(tn):
                            kt = t0 + k
                            ins = t.matmul(ps_s[nmm % 4][:, k * 128:(k + 1) * 128],
                                           kb[idx % 4][:, kt * 128:(kt + 1) * 128],
                                           qt_sb[:, h * 128:(h + 1) * 128],
                                           start=True, stop=True)
                        ins.then_inc(mmd, 1)
                        nmm += 1
                    if h >= 1:
                        lag_av(idx - 1)
                lag_av(j * H + H - 1)

        @block.gpsimd
        def _(g):
            g = cx.G(g)
            cx.wait_bar(g)
            g.dma_start(out=idn[:, :], in_=identb).then_inc(ld, 16)
            g.dma_start(out=mskT[:, :].rearrange("p (a k) -> p a k", k=128),
                        in_=maskabT.rearrange("(a p) k -> p a k", p=128)).then_inc(ld, 16)
            for j in range(NT):
                w = EXT[j] * 128
                if j >= 2:
                    g.wait_ge(trd, 2 * (j - 1))
                g.dma_start(out=qbuf[j % 2][:, :], in_=q_tok[j * 128:(j + 1) * 128, :]).then_inc(ld, 16)
                for h in range(H):
                    idx = j * H + h
                    if idx >= 4:
                        g.wait_ge(avd, idx - 3)
                    g.dma_start(
                        out=kb[idx % 4][:, 0:2 * w].rearrange("p (a w) -> p a w", a=2),
                        in_=gk.rearrange("(a r) t -> a r t", a=2)[:, h * 128:(h + 1) * 128, 0:w]
                        .rearrange("a p w -> p a w"),
                    ).then_inc(ld, 16)
                    for par in range(2):
                        g.dma_start(
                            out=vb[idx % 4][:, par * EXT[j] * 129:(par + 1) * EXT[j] * 129]
                            .rearrange("p (t c) -> p t c", c=129)[:, :, 0:128],
                            in_=gv[par * T: par * T + w, h * 128:(h + 1) * 128]
                            .rearrange("(t p) d -> p t d", p=128),
                        ).then_inc(ld, 16)

        @block.sync
        def _(sy):
            sy = cx.SY(sy)
            cx.wait_bar(sy)
            for j in range(NT):
                sy.wait_ge(vss, ev[f"osd_{(j + 1) * H - 1}"])
                sy.dma_start(out=attn_tok[j * 128:(j + 1) * 128, :],
                             in_=ostage[j % 2][:, :]).then_inc(st, 16)
    cx.end_phase([(ld, ld_kv[NT - 1][H - 1]), (trd, 2 * NT), (mmd, nmm_total),
                  (avd, NT * H), (st, NT)])


def emit_oproj(cx, attn_tok, w, aT):
    nc = cx.nc
    es = ExitStack()
    with es:
        at_sb = es.enter_context(cx.sbuf("at_sb", [128, CH * T], BF16))
        w_sb = es.enter_context(cx.sbuf("w_sb", [128, CH * D], BF16))
        stage = es.enter_context(cx.sbuf("stage", [128, 2 * T], BF16))
        ps = [es.enter_context(cx.psum(f"ps{i}", [128, T], F32)) for i in range(2)]
        ld = cx.dsem("ld")
        tld = cx.dsem("tld")
        mmg = cx.sem("mmg")
        pass
        st = cx.dsem("st")
        block = cx.block
        vss = cx.vss
        sss = cx.sss
        SS0 = cx.sw_n()

        @block.scalar
        def _(s):
            s = cx.S(s)
            cx.wait_bar(s)
            for ot in range(16):
                s.wait_ge(mmg, ot + 1)
                if ot >= 2:
                    dw(s, st, (ot - 1))
                s.activation(stage[:, (ot % 2) * T:(ot % 2 + 1) * T],
                             ps[ot % 2][:, :], AF.Copy)

        @block.tensor
        def _(t):
            cx.wait_bar(t)
            dw(t, ld, 16)
            dw(t, tld, 16)
            for ot in range(16):
                if ot >= 2:
                    t.wait_ge(sss, SS0 + ot - 1)
                p = ps[ot % 2]
                ins = None
                for c in range(CH):
                    lhsT = w_sb[:, c * D + ot * 128: c * D + (ot + 1) * 128]
                    for q4 in range(4):
                        ins = t.matmul(p[:, q4 * 512:(q4 + 1) * 512], lhsT,
                                       at_sb[:, c * T + q4 * 512: c * T + (q4 + 1) * 512],
                                       start=(c == 0), stop=(c == CH - 1))
                ins.then_inc(mmg, 1)

        @block.sync
        def _(sy):
            sy = cx.SY(sy)
            cx.wait_bar(sy)
            for c in range(CH):
                sy.dma_start(out=at_sb[:, c * T:(c + 1) * T],
                             in_=attn_tok[:, c * 128:(c + 1) * 128], transpose=True).then_inc(tld, 16)

        @block.gpsimd
        def _(g):
            g = cx.G(g)
            cx.wait_bar(g)
            for c in range(CH):
                g.dma_start(out=w_sb[:, c * D:(c + 1) * D],
                            in_=w[c * 128:(c + 1) * 128, :]).then_inc(ld, 16)
            for ot in range(16):
                g.wait_ge(sss, SS0 + ot + 1)
                g.dma_start(out=aT[ot * 128:(ot + 1) * 128, :],
                            in_=stage[:, (ot % 2) * T:(ot % 2 + 1) * T]).then_inc(st, 16)
    cx.end_phase([(ld, 16), (tld, 16), (mmg, 16), (st, 16)])



def emit_attn1b1(cx, q_tok, gk, maskab, identb, pt_dram):
    nc = cx.nc
    es = ExitStack()
    with es:
        idn = es.enter_context(cx.sbuf("idn", [128, 128], BF16))
        msk = es.enter_context(cx.sbuf("msk", [128, 256], F32))
        qbuf = [es.enter_context(cx.sbuf(f"qbuf{i}", [128, D], BF16)) for i in range(2)]
        qt_sb = es.enter_context(cx.sbuf("qt_sb", [128, H * 128], BF16))
        kb = [es.enter_context(cx.sbuf(f"kb{i}", [128, 2 * T], BF16)) for i in range(4)]
        sc_sb = [es.enter_context(cx.sbuf(f"sc_sb{i}", [128, 4096], F32)) for i in range(2)]
        p_sb = [es.enter_context(cx.sbuf(f"p_sb{i}", [128, 4096], BF16)) for i in range(2)]
        pa_sb = [es.enter_context(cx.sbuf(f"pa_sb{i}", [128, 4096], F32)) for i in range(2)]
        nm = [es.enter_context(cx.sbuf(f"nm{i}", [128, 1], F32)) for i in range(2)]
        ssum = [es.enter_context(cx.sbuf(f"ssum{i}", [128, 1], F32)) for i in range(2)]
        rs = [es.enter_context(cx.sbuf(f"rs{i}", [128, 1], F32)) for i in range(2)]
        ps_pt = es.enter_context(cx.psum("ps_pt", [128, H * 128], BF16))
        ps_s = [es.enter_context(cx.psum(f"ps_s{i}", [128, 512], F32)) for i in range(4)]
        ld = cx.dsem("ld", w=6)
        trd = cx.sem("trd")
        mmd = cx.sem("mmd")
        st = cx.dsem("st")
        block = cx.block
        vss = cx.vss
        sss = cx.sss
        SS0 = cx.sw_n()
        ev = {}

        CKS = [_chunks(EXT[j] * 128) for j in range(NT)]
        nch = [2 * len(CKS[j]) for j in range(NT)]
        ld_q, ld_kv = [], []
        n = 2
        for j in range(NT):
            n += 1
            ld_q.append(n)
            row = []
            for h in range(H):
                n += 1
                row.append(n)
            ld_kv.append(row)
        sss_afterE = [0] * (NT * H)
        sss_afterX = [0] * (NT * H)
        c = 0
        for idx in range(NT * H):
            c += nch[idx // H]
            sss_afterE[idx] = c
            if idx >= 1:
                c += 1
                sss_afterX[idx - 1] = c
        c += 1
        sss_afterX[NT * H - 1] = c
        mm_idx = []
        c = 0
        for j in range(NT):
            for h in range(H):
                c += nch[j]
                mm_idx.append(c)

        @block.vector
        def _(v):
            v = cx.V(v)
            cx.wait_bar(v)

            def lag(pidx):
                pj, phh = divmod(pidx, H)
                pw = EXT[pj] * 128
                v.wait_ge(sss, SS0 + sss_afterX[pidx])
                v.reciprocal(rs[pidx % 2][:, 0:1], ssum[pidx % 2][:, 0:1])
                if phh == 0 and pj >= 2:
                    dw(v, st, (pj - 1))
                pa = pa_sb[pj % 2]
                if phh == 0:
                    v.tensor_scalar(pa[:, 0:2 * pw], p_sb[pidx % 2][:, 0:2 * pw],
                                    rs[pidx % 2][:, 0:1], None, ALU.mult)
                else:
                    v.scalar_tensor_tensor(pa[:, 0:2 * pw], p_sb[pidx % 2][:, 0:2 * pw],
                                           rs[pidx % 2][:, 0:1], pa[:, 0:2 * pw],
                                           ALU.mult, ALU.add)
                ev[f"acd_{pidx}"] = v.n

            for j in range(NT):
                w = EXT[j] * 128
                v.wait_ge(trd, j + 1)
                v.tensor_copy(qt_sb[:, :], ps_pt[:, :])
                ev[f"qtev_{j}"] = v.n
                for h in range(H):
                    idx = j * H + h
                    v.wait_ge(sss, SS0 + sss_afterE[idx])
                    sc = sc_sb[idx % 2]
                    dt0 = slice((EXT[j] - 1) * 128, EXT[j] * 128)
                    v.tensor_add(sc[:, dt0], sc[:, dt0], msk[:, 0:128])
                    dt1 = slice(w + (EXT[j] - 1) * 128, w + EXT[j] * 128)
                    v.tensor_add(sc[:, dt1], sc[:, dt1], msk[:, 128:256])
                    v.tensor_reduce(nm[idx % 2][:, 0:1], sc[:, 0:2 * w], AX.X, ALU.max, negate=True)
                    v.tensor_scalar_mul(nm[idx % 2][:, 0:1], nm[idx % 2][:, 0:1], SCALE)
                    ev[f"stats_{idx}"] = v.n
                    if idx >= 1:
                        lag(idx - 1)
            lag(NT * H - 1)

        @block.scalar
        def _(s):
            s = cx.S(s)
            cx.wait_bar(s)
            nmm = 0

            def do_exp(pidx):
                pw = EXT[pidx // H] * 128
                s.wait_ge(vss, ev[f"stats_{pidx}"])
                if pidx >= 2:
                    s.wait_ge(vss, ev[f"acd_{pidx - 2}"])
                s.activation(p_sb[pidx % 2][:, 0:2 * pw], sc_sb[pidx % 2][:, 0:2 * pw],
                             AF.Exp, scale=SCALE, bias=nm[pidx % 2][:, 0:1],
                             accum_out=ssum[pidx % 2][:, 0:1])

            for j in range(NT):
                w = EXT[j] * 128
                for h in range(H):
                    idx = j * H + h
                    sc = sc_sb[idx % 2]
                    for par in range(2):
                        for (off, cw) in CKS[j]:
                            s.wait_ge(mmd, nmm + 1)
                            s.activation(sc[:, par * w + off: par * w + off + cw],
                                         ps_s[nmm % 4][:, 0:cw], AF.Copy)
                            ev[f"sev_{nmm}"] = s.n
                            nmm += 1
                    if idx >= 1:
                        do_exp(idx - 1)
            do_exp(NT * H - 1)

        @block.tensor
        def _(t):
            cx.wait_bar(t)
            nmm = 0
            for j in range(NT):
                w = EXT[j] * 128
                dw(t, ld, ld_q[j])
                if j > 0:
                    t.wait_ge(vss, ev[f"qtev_{j - 1}"])
                ins = None
                for h in range(H):
                    ins = t.transpose(ps_pt[:, h * 128:(h + 1) * 128],
                                      qbuf[j % 2][:, h * 128:(h + 1) * 128], idn[:, :])
                ins.then_inc(trd, 1)
                t.wait_ge(vss, ev[f"qtev_{j}"])
                for h in range(H):
                    idx = j * H + h
                    dw(t, ld, ld_kv[j][h])
                    for par in range(2):
                        for (off, cw) in CKS[j]:
                            if nmm >= 4:
                                t.wait_ge(sss, ev[f"sev_{nmm - 4}"])
                            t.matmul(ps_s[nmm % 4][:, 0:cw],
                                     qt_sb[:, h * 128:(h + 1) * 128],
                                     kb[idx % 4][:, par * w + off: par * w + off + cw],
                                     start=True, stop=True).then_inc(mmd, 1)
                            nmm += 1

        @block.gpsimd
        def _(g):
            g = cx.G(g)
            cx.wait_bar(g)
            g.dma_start(out=idn[:, :], in_=identb).then_inc(ld, 16)
            g.dma_start(out=msk[:, :].rearrange("p (a k) -> p a k", k=128),
                        in_=maskab.rearrange("(a p) k -> p a k", p=128)).then_inc(ld, 16)
            for j in range(NT):
                w = EXT[j] * 128
                if j >= 2:
                    g.wait_ge(trd, j - 1)
                g.dma_start(out=qbuf[j % 2][:, :], in_=q_tok[j * 128:(j + 1) * 128, :]).then_inc(ld, 16)
                for h in range(H):
                    idx = j * H + h
                    if idx >= 4:
                        g.wait_ge(mmd, mm_idx[idx - 4])
                    g.dma_start(
                        out=kb[idx % 4][:, 0:2 * w].rearrange("p (a w) -> p a w", a=2),
                        in_=gk.rearrange("(a r) t -> a r t", a=2)[:, h * 128:(h + 1) * 128, 0:w]
                        .rearrange("a p w -> p a w"),
                    ).then_inc(ld, 16)
                if j >= 1:
                    pw = EXT[j - 1] * 128
                    g.wait_ge(vss, ev[f"acd_{j * H - 1}"])
                    g.dma_start(out=pt_dram[(j - 1) * 128:j * 128, 0:2 * pw],
                                in_=pa_sb[(j - 1) % 2][:, 0:2 * pw]).then_inc(st, 16)
            g.wait_ge(vss, ev[f"acd_{NT * H - 1}"])
            g.dma_start(out=pt_dram[(NT - 1) * 128:NT * 128, 0:2 * EXT[NT - 1] * 128],
                        in_=pa_sb[(NT - 1) % 2][:, 0:2 * EXT[NT - 1] * 128]).then_inc(st, 16)

    nmm_total = sum(16 * 2 * len(_chunks(EXT[j] * 128)) for j in range(NT))
    cx.end_phase([(ld, ld_kv[NT - 1][H - 1]), (trd, NT), (mmd, nmm_total), (st, NT)])


def emit_attn1b2(cx, pt_dram, gv, identb, attn_tok):
    nc = cx.nc
    es = ExitStack()
    with es:
        idn = es.enter_context(cx.sbuf("idn", [128, 128], BF16))
        pbuf = [es.enter_context(cx.sbuf(f"pbuf{i}", [128, 4096], BF16)) for i in range(2)]
        pt_sb = es.enter_context(cx.sbuf("pt_sb", [128, 4096], BF16))
        vbuf = [es.enter_context(cx.sbuf(f"vbuf{i}", [128, D], BF16)) for i in range(2)]
        ostage = [es.enter_context(cx.sbuf(f"ostage{i}", [128, D], BF16)) for i in range(2)]
        ps_pt = es.enter_context(cx.psum("ps_pt", [128, 4096], BF16))
        ps_o = es.enter_context(cx.psum("ps_o", [128, D], F32))
        ld = cx.dsem("ld")
        vld = cx.dsem("vld")
        mmv = cx.sem("mmv")
        trd = cx.sem("trd")
        pass
        st = cx.dsem("st")
        block = cx.block
        vss = cx.vss
        sss = cx.sss
        SS0 = cx.sw_n()
        ev = {}

        cum_mmv = []
        tot = 0
        for j in range(NT):
            tot += 2 * EXT[j]
            cum_mmv.append(tot)

        @block.vector
        def _(v):
            v = cx.V(v)
            cx.wait_bar(v)
            for j in range(NT):
                w = EXT[j] * 128
                v.wait_ge(trd, j + 1)
                v.tensor_copy(pt_sb[:, 0:2 * w], ps_pt[:, 0:2 * w])
                ev[f"ptev_{j}"] = v.n
                v.wait_ge(mmv, cum_mmv[j])
                if j >= 2:
                    dw(v, st, (j - 1))
                v.tensor_copy(ostage[j % 2][:, :], ps_o[:, :])
                ev[f"osd_{j}"] = v.n

        @block.tensor
        def _(t):
            cx.wait_bar(t)
            nv = 0
            for j in range(NT):
                w = EXT[j] * 128
                dw(t, ld, (j + 2))
                if j > 0:
                    t.wait_ge(vss, ev[f"ptev_{j - 1}"])
                ins = None
                for kt in range(2 * EXT[j]):
                    ins = t.transpose(ps_pt[:, kt * 128:(kt + 1) * 128],
                                      pbuf[j % 2][:, kt * 128:(kt + 1) * 128], idn[:, :])
                ins.then_inc(trd, 1)
                t.wait_ge(vss, ev[f"ptev_{j}"])
                if j > 0:
                    t.wait_ge(vss, ev[f"osd_{j - 1}"])
                nkt = 2 * EXT[j]
                for kt in range(nkt):
                    dw(t, vld, (nv + 1))
                    ins = None
                    for q4 in range(4):
                        ins = t.matmul(ps_o[:, q4 * 512:(q4 + 1) * 512],
                                       pt_sb[:, kt * 128:(kt + 1) * 128],
                                       vbuf[nv % 2][:, q4 * 512:(q4 + 1) * 512],
                                       start=(kt == 0), stop=(kt == nkt - 1))
                    ins.then_inc(mmv, 1)
                    nv += 1

        @block.sync
        def _(sy):
            sy = cx.SY(sy)
            cx.wait_bar(sy)
            nv = 0
            for j in range(NT):
                for kt in range(2 * EXT[j]):
                    par, tt = (0, kt) if kt < EXT[j] else (1, kt - EXT[j])
                    if nv >= 2:
                        sy.wait_ge(mmv, nv - 1)
                    sy.dma_start(out=vbuf[nv % 2][:, :],
                                 in_=gv[par * T + tt * 128: par * T + (tt + 1) * 128, :]
                                 ).then_inc(vld, 16)
                    nv += 1

        @block.gpsimd
        def _(g):
            g = cx.G(g)
            cx.wait_bar(g)
            g.dma_start(out=idn[:, :], in_=identb).then_inc(ld, 16)
            for j in range(NT):
                w = EXT[j] * 128
                if j >= 2:
                    g.wait_ge(trd, j - 1)
                g.dma_start(out=pbuf[j % 2][:, 0:2 * w],
                            in_=pt_dram[j * 128:(j + 1) * 128, 0:2 * w]).then_inc(ld, 16)
                g.wait_ge(vss, ev[f"osd_{j}"])
                g.dma_start(out=attn_tok[j * 128:(j + 1) * 128, :],
                            in_=ostage[j % 2][:, :]).then_inc(st, 16)
    cx.end_phase([(ld, 17), (vld, 272), (mmv, 272), (trd, NT), (st, NT)])



def emit_ffn1(cx, xnT, wbase, bbaseT, a_s, b_s, a_r, b_r, wrout, brout, identf, esel, xT, aT, outT):
    nc = cx.nc
    es = ExitStack()
    with es:
        xn_sb = es.enter_context(cx.sbuf("xn_sb", [128, CH * T], BF16))
        wbb = [es.enter_context(cx.sbuf(f"wbb{i}", [128, CH * 128], BF16)) for i in range(2)]
        bb_sb = es.enter_context(cx.sbuf("bb_sb", [128, 16], F32))
        as_sb = es.enter_context(cx.sbuf("as_sb", [128, CH * R], BF16))
        bs_sb = es.enter_context(cx.sbuf("bs_sb", [16, D], BF16))
        ar_sb = es.enter_context(cx.sbuf("ar_sb", [128, E * CH * R], BF16))
        br_sb = es.enter_context(cx.sbuf("br_sb", [16, E * D], BF16))
        wr_sb = es.enter_context(cx.sbuf("wr_sb", [128, CH * E], BF16))
        bro_b = es.enter_context(cx.sbuf("bro_b", [128, E], F32))
        idf = es.enter_context(cx.sbuf("idf", [128, 128], F32))
        esel_sb = es.enter_context(cx.sbuf("esel_sb", [6, E * 16], F32))
        g_sb = es.enter_context(cx.sbuf("g_sb", [128, E], F32))
        ge_sb = es.enter_context(cx.sbuf("ge_sb", [128, E], F32))
        gn_sb = es.enter_context(cx.sbuf("gn_sb", [128, E], F32))
        m1 = es.enter_context(cx.sbuf("m1", [128, 1], F32))
        s1 = es.enter_context(cx.sbuf("s1", [128, 1], F32))
        r1 = es.enter_context(cx.sbuf("r1", [128, 1], F32))
        sel = es.enter_context(cx.sbuf("sel", [128, E], F32))
        g2 = es.enter_context(cx.sbuf("g2", [128, E], F32))
        wgt = es.enter_context(cx.sbuf("wgt", [128, NT * E], F32))
        wt_sb = es.enter_context(cx.sbuf("wt_sb", [6, T], F32))
        tmp_sb = es.enter_context(cx.sbuf("tmp_sb", [16, T], BF16))
        h_sb = es.enter_context(cx.sbuf("h_sb", [16, E * T], BF16))
        xc = es.enter_context(cx.sbuf("xc", [128, T], F32))
        ac = es.enter_context(cx.sbuf("ac", [128, T], BF16))
        o_sb = es.enter_context(cx.sbuf("o_sb", [128, T], F32))
        ld = cx.dsem("ld")
        wld = cx.dsem("wld")
        cld = cx.dsem("cld")
        rta = cx.sem("rta")
        trd = cx.sem("trd")
        cmm = cx.sem("cmm")
        wbm = cx.sem("wbm")
        ymm = cx.sem("ymm")
        pass
        pass
        st = cx.dsem("st")
        block = cx.block
        vss = cx.vss
        sss = cx.sss
        SS0 = cx.sw_n()
        ev = {}
        NLD = 35
        NGRP = 4 * (1 + E)

        ps_g = es.enter_context(cx.psum("ps_g", [128, NT * E], F32))

        with cx.psum("ps_w", [6, T], F32) as ps_w, \
                nc.psum_tensor("ps_t0", [16, 512], F32) as ps_t0, \
                nc.psum_tensor("ps_t1", [16, 512], F32) as ps_t1:
            ps_t = [ps_t0, ps_t1]

            @block.vector
            def _(v):
                v = cx.V(v)
                cx.wait_bar(v)
                v.wait_ge(rta, 1)
                for tt in range(NT):
                    v.tensor_add(g_sb[:, :], ps_g[:, tt * E:(tt + 1) * E], bro_b[:, :])
                    v.tensor_reduce(m1[:, 0:1], g_sb[:, :], AX.X, ALU.max, negate=True)
                    ev[f"rt1_{tt}"] = v.n
                    v.wait_ge(sss, SS0 + tt + 1)
                    v.reciprocal(r1[:, 0:1], s1[:, 0:1])
                    v.tensor_scalar(gn_sb[:, :], ge_sb[:, :], r1[:, 0:1], None, ALU.mult)
                    v.tensor_reduce(m1[:, 0:1], gn_sb[:, :], AX.X, ALU.max)
                    v.tensor_scalar(sel[:, :], gn_sb[:, :], m1[:, 0:1], None, ALU.is_ge)
                    v.scalar_tensor_tensor(g2[:, :], sel[:, :], -1e9, gn_sb[:, :], ALU.mult, ALU.add)
                    v.tensor_reduce(m1[:, 0:1], g2[:, :], AX.X, ALU.max)
                    v.tensor_scalar(sel[:, :], gn_sb[:, :], m1[:, 0:1], None, ALU.is_ge)
                    v.tensor_mul(wgt[:, tt * E:(tt + 1) * E], gn_sb[:, :], sel[:, :])
                    ev[f"wgt_{tt}"] = v.n
                v.wait_ge(trd, 1)
                v.tensor_copy(wt_sb[0:6, :], ps_w[0:6, :])
                ev["wtev"] = v.n
                for ngrp in range(NGRP):
                    src, q4 = divmod(ngrp, 4)
                    v.wait_ge(cmm, ngrp + 1)
                    if src == 0:
                        dst = tmp_sb[0:16, q4 * 512:(q4 + 1) * 512]
                    else:
                        e = src - 1
                        dst = h_sb[0:16, e * T + q4 * 512: e * T + (q4 + 1) * 512]
                    v.tensor_copy(dst, ps_t[ngrp % 2][0:16, :])
                    ev[f"cev_{ngrp}"] = v.n

            @block.scalar
            def _(s):
                s = cx.S(s)
                cx.wait_bar(s)
                for tt in range(NT):
                    s.wait_ge(vss, ev[f"rt1_{tt}"])
                    s.activation(ge_sb[:, :], g_sb[:, :], AF.Exp, bias=m1[:, 0:1],
                                 accum_out=s1[:, 0:1])

            @block.tensor
            def _(t):
                cx.wait_bar(t)
                dw(t, ld, NLD)
                ins = None
                for tt in range(NT):
                    for c in range(CH):
                        ins = t.matmul(ps_g[:, tt * E:(tt + 1) * E],
                                       xn_sb[:, c * T + tt * 128: c * T + (tt + 1) * 128],
                                       wr_sb[:, c * E:(c + 1) * E],
                                       start=(c == 0), stop=(c == CH - 1))
                ins.then_inc(rta, 1)
                t.wait_ge(vss, ev[f"wgt_{NT - 1}"])
                ins = None
                for tt in range(NT):
                    ins = t.transpose(ps_w[:, tt * 128:(tt + 1) * 128],
                                      wgt[:, tt * E:(tt + 1) * E], idf[:, :])
                ins.then_inc(trd, 1)
                for ngrp in range(NGRP):
                    src, q4 = divmod(ngrp, 4)
                    if ngrp >= 2:
                        t.wait_ge(vss, ev[f"cev_{ngrp - 2}"])
                    ins = None
                    for c in range(CH):
                        if src == 0:
                            lhsT = as_sb[:, c * R:(c + 1) * R]
                        else:
                            e = src - 1
                            lhsT = ar_sb[:, e * CH * R + c * R: e * CH * R + (c + 1) * R]
                        ins = t.matmul(ps_t[ngrp % 2][0:16, :], lhsT,
                                       xn_sb[:, c * T + q4 * 512: c * T + (q4 + 1) * 512],
                                       start=(c == 0), stop=(c == CH - 1))
                    ins.then_inc(cmm, 1)

        with cx.psum("ps_wb", [16, T], F32) as ps_wb:

            @block.vector
            def _(v):
                v = cx.V(v)
                for e in range(E):
                    v.wait_ge(wbm, e + 1)
                    v.tensor_tensor(h_sb[0:16, e * T:(e + 1) * T], h_sb[0:16, e * T:(e + 1) * T],
                                    ps_wb[0:16, :], ALU.mult)
                    ev[f"hwd_{e}"] = v.n

            @block.tensor
            def _(t):
                cx.wait_bar(t)
                t.wait_ge(vss, ev["wtev"])
                t.wait_ge(vss, ev[f"cev_{NGRP - 1}"])
                for e in range(E):
                    if e > 0:
                        t.wait_ge(vss, ev[f"hwd_{e - 1}"])
                    ins = None
                    for q4 in range(4):
                        ins = t.matmul(ps_wb[0:16, q4 * 512:(q4 + 1) * 512],
                                       esel_sb[0:6, e * 16:(e + 1) * 16],
                                       wt_sb[0:6, q4 * 512:(q4 + 1) * 512],
                                       start=True, stop=True)
                    ins.then_inc(wbm, 1)

        with cx.psum("psy0", [128, 512], F32) as psy0, \
                nc.psum_tensor("psy1", [128, 512], F32) as psy1:
            psy = [psy0, psy1]

            @block.vector
            def _(v):
                v = cx.V(v)
                nn = 0
                for ft in range(CH):
                    dw(v, cld, 2 * (ft + 1))
                    if ft >= 1:
                        dw(v, st, ft)
                    for q4 in range(4):
                        v.wait_ge(ymm, nn + 1)
                        tc = slice(q4 * 512, (q4 + 1) * 512)
                        osl = o_sb[:, tc]
                        v.tensor_scalar(osl, psy[nn % 2][:, :], bb_sb[:, ft:ft + 1],
                                        None, ALU.add)
                        ev[f"yev_{nn}"] = v.n
                        v.tensor_add(osl, osl, ac[:, tc])
                        v.tensor_add(osl, osl, xc[:, tc])
                        nn += 1
                    ev[f"od_{ft}"] = v.n

            @block.tensor
            def _(t):
                cx.wait_bar(t)
                t.wait_ge(vss, ev[f"hwd_{E - 1}"])
                nn = 0
                for ft in range(CH):
                    dw(t, wld, (ft + 1))
                    for q4 in range(4):
                        if nn >= 2:
                            t.wait_ge(vss, ev[f"yev_{nn - 2}"])
                        p = psy[nn % 2]
                        tc = slice(q4 * 512, (q4 + 1) * 512)
                        ins = None
                        for c in range(CH):
                            t.matmul(p[:, :], wbb[ft % 2][:, c * 128:(c + 1) * 128],
                                     xn_sb[:, c * T + q4 * 512: c * T + (q4 + 1) * 512],
                                     start=(c == 0), stop=False)
                        t.matmul(p[:, :], bs_sb[0:16, ft * 128:(ft + 1) * 128],
                                 tmp_sb[0:16, tc], start=False, stop=False)
                        for e in range(E):
                            ins = t.matmul(p[:, :], br_sb[0:16, e * D + ft * 128: e * D + (ft + 1) * 128],
                                           h_sb[0:16, e * T + q4 * 512: e * T + (q4 + 1) * 512],
                                           start=False, stop=(e == E - 1))
                        ins.then_inc(ymm, 1)
                        nn += 1

            @block.gpsimd
            def _(g):
                g = cx.G(g)
                cx.wait_bar(g)
                g.dma_start(out=idf[:, :], in_=identf).then_inc(ld, 16)
                for c in range(CH):
                    g.dma_start(out=xn_sb[:, c * T:(c + 1) * T],
                                in_=xnT[c * 128:(c + 1) * 128, :]).then_inc(ld, 16)
                g.dma_start(out=bb_sb[:, :], in_=bbaseT).then_inc(ld, 16)
                g.dma_start(out=as_sb[:, :].rearrange("p (c r) -> p c r", r=R),
                            in_=a_s.rearrange("(c p) r -> p c r", p=128)).then_inc(ld, 16)
                g.dma_start(out=bs_sb[0:16, :], in_=b_s).then_inc(ld, 16)
                for e in range(E):
                    g.dma_start(out=ar_sb[:, e * CH * R:(e + 1) * CH * R].rearrange(
                                    "p (c r) -> p c r", r=R),
                                in_=a_r[e, :, :].rearrange("(c p) r -> p c r", p=128)).then_inc(ld, 16)
                for e in range(E):
                    g.dma_start(out=br_sb[0:16, e * D:(e + 1) * D], in_=b_r[e, :, :]).then_inc(ld, 16)
                g.dma_start(out=wr_sb[:, :].rearrange("p (c e) -> p c e", e=E),
                            in_=wrout.rearrange("(c p) e -> p c e", p=128)).then_inc(ld, 16)
                g.dma_start(out=bro_b[:, :], in_=brout.broadcast_to([128, E])).then_inc(ld, 16)
                g.dma_start(out=esel_sb[0:6, :], in_=esel).then_inc(ld, 16)
                for ft in range(CH):
                    if ft >= 2:
                        g.wait_ge(ymm, 4 * (ft - 1))
                    g.dma_start(out=wbb[ft % 2][:, :].rearrange("p (c f) -> p c f", f=128),
                                in_=wbase[:, ft * 128:(ft + 1) * 128].rearrange(
                                    "(c p) f -> p c f", p=128)).then_inc(wld, 16)
                    if ft > 0:
                        g.wait_ge(vss, ev[f"od_{ft - 1}"])
                    g.dma_start(out=xc[:, :], in_=xT[ft * 128:(ft + 1) * 128, :]).then_inc(cld, 16)
                    g.dma_start(out=ac[:, :], in_=aT[ft * 128:(ft + 1) * 128, :]).then_inc(cld, 16)
                    g.wait_ge(vss, ev[f"od_{ft}"])
                    g.dma_start(out=outT[ft * 128:(ft + 1) * 128, :],
                                in_=o_sb[:, :]).then_inc(st, 16)
    cx.end_phase([(ld, 35), (wld, 16), (cld, 32), (rta, 1), (trd, 1),
                  (cmm, 28), (wbm, 6), (ymm, 64), (st, 1)])



def emit_ag(cx, ins_outs):
    nc = cx.nc
    cc = cx.sem("cc")
    block = cx.block

    @block.gpsimd
    def _(g):
        g = cx.G(g)
        cx.wait_bar(g)
        for (src_t, dst_t) in ins_outs:
            g.collective_compute(
                "AllGather", ALU.bypass,
                replica_groups=[[0, 1], [2, 3], [4, 5], [6, 7]],
                ins=[src_t.ap()], outs=[dst_t.ap()],
            ).then_inc(cc, 1)
        g.wait_ge(cc, len(ins_outs))
    cx.end_phase([(cc, len(ins_outs))])


def build_fused():
    nc = _nc()
    ap = lambda h: h.ap()
    xT = nc.dram_tensor("xT", [D, T], F32, kind="ExternalInput").ap()
    wproj = nc.dram_tensor("wproj", [D, 3 * D], F32, kind="ExternalInput").ap()
    wout = nc.dram_tensor("wout", [D, D], F32, kind="ExternalInput").ap()
    w1T = nc.dram_tensor("w1T", [D, HID], F32, kind="ExternalInput").ap()
    w2T = nc.dram_tensor("w2T", [D, HID], F32, kind="ExternalInput").ap()
    w3T = nc.dram_tensor("w3T", [HID, D], F32, kind="ExternalInput").ap()
    wqkv1 = nc.dram_tensor("wqkv1", [D, 3 * D], F32, kind="ExternalInput").ap()
    wo1 = nc.dram_tensor("wo1", [D, D], F32, kind="ExternalInput").ap()
    wbase = nc.dram_tensor("wbase", [D, D], F32, kind="ExternalInput").ap()
    bbaseT = nc.dram_tensor("bbaseT", [128, 16], F32, kind="ExternalInput").ap()
    a_s = nc.dram_tensor("a_s", [D, R], F32, kind="ExternalInput").ap()
    b_s = nc.dram_tensor("b_s", [R, D], F32, kind="ExternalInput").ap()
    a_r = nc.dram_tensor("a_r", [E, D, R], F32, kind="ExternalInput").ap()
    b_r = nc.dram_tensor("b_r", [E, R, D], F32, kind="ExternalInput").ap()
    wrout = nc.dram_tensor("wrout", [D, E], F32, kind="ExternalInput").ap()
    brout = nc.dram_tensor("brout", [1, E], F32, kind="ExternalInput").ap()
    identb = nc.dram_tensor("identb", [128, 128], BF16, kind="ExternalInput").ap()
    identf = nc.dram_tensor("identf", [128, 128], F32, kind="ExternalInput").ap()
    esel = nc.dram_tensor("esel", [E, E * 16], F32, kind="ExternalInput").ap()
    maskab = nc.dram_tensor("maskab", [256, 128], F32, kind="ExternalInput").ap()
    maskabT = nc.dram_tensor("maskabT", [256, 128], F32, kind="ExternalInput").ap()
    outT = nc.dram_tensor("outT", [D, T], F32, kind="ExternalOutput").ap()

    thr = nc.dram_tensor("thr_i", [1, 8], F32, kind="Internal")
    xn = nc.dram_tensor("xn_i", [D, T], BF16, kind="Internal")
    q_t = nc.dram_tensor("q_i", [T, D], BF16, kind="Internal")
    k_own = nc.dram_tensor("k_i", [D, T], BF16, kind="Internal")
    v_own = nc.dram_tensor("v_i", [T, D], BF16, kind="Internal")
    gk = nc.dram_tensor("gk_i", [2 * D, T], BF16, kind="Internal")
    gv = nc.dram_tensor("gv_i", [2 * T, D], BF16, kind="Internal")
    att = nc.dram_tensor("att_i", [T, D], BF16, kind="Internal")
    aTb = nc.dram_tensor("aT_i", [D, T], BF16, kind="Internal")
    hid = nc.dram_tensor("hid_i", [HID, T], BF16, kind="Internal")
    x1 = nc.dram_tensor("x1_i", [D, T], F32, kind="Internal")
    pt = nc.dram_tensor("pt_i", [T, 4096], BF16, kind="Internal")

    es = ExitStack()
    with es:
        block = es.enter_context(nc.Block())
        cx = FCtx(nc, es, block)
        emit_gamma(cx, w1T, w2T, w3T, thr.ap())
        emit_ln(cx, xT, xn.ap())
        emit_qkv(cx, xn.ap(), wproj, q_t.ap(), k_own.ap(), v_own.ap())
        emit_ag(cx, [(k_own, gk), (v_own, gv)])
        emit_attn0(cx, q_t.ap(), gk.ap(), gv.ap(), maskabT, identb, att.ap())
        emit_oproj(cx, att.ap(), wout, aTb.ap())
        emit_ffna(cx, xn.ap(), w1T, w2T, thr.ap(), hid.ap())
        emit_ffnb(cx, hid.ap(), w3T, thr.ap(), xT, aTb.ap(), x1.ap())
        emit_ln(cx, x1.ap(), xn.ap())
        emit_qkv(cx, xn.ap(), wqkv1, q_t.ap(), k_own.ap(), v_own.ap())
        emit_ag(cx, [(k_own, gk), (v_own, gv)])
        emit_attn1b1(cx, q_t.ap(), gk.ap(), maskab, identb, pt.ap())
        emit_attn1b2(cx, pt.ap(), gv.ap(), identb, att.ap())
        emit_oproj(cx, att.ap(), wo1, aTb.ap())
        emit_ffn1(cx, xn.ap(), wbase, bbaseT, a_s, b_s, a_r, b_r, wrout, brout,
                  identf, esel, x1.ap(), aTb.ap(), outT)
    return nc


_NCS = {}


def _get(name, builder):
    if name not in _NCS:
        _NCS[name] = builder()
    return _NCS[name]


def _f32(a):
    return np.ascontiguousarray(a, dtype=np.float32)


def kernel(L0_attn_x, L0_attn_w_project, L0_attn_w_out, L0_ffn_w1, L0_ffn_w2, L0_ffn_w3,
           L1_attn_W_q, L1_attn_W_k, L1_attn_W_v, L1_attn_W_o,
           L1_ffn_W_base, L1_ffn_b_base, L1_ffn_A_s, L1_ffn_B_s, L1_ffn_A_r, L1_ffn_B_r,
           L1_ffn_W_router, L1_ffn_b_router):
    x = np.asarray(L0_attn_x, dtype=np.float32)
    B = x.shape[0]

    xTs = []
    for c in range(8):
        seq, par = c // 2, c % 2
        rows = x[seq].reshape(32, 128, D)[par::2].reshape(T, D)
        xTs.append(_f32(rows.T))

    identb = np.eye(128, dtype=ml_dtypes.bfloat16)
    identf = np.eye(128, dtype=np.float32)
    tri = np.where(np.arange(128)[None, :] <= np.arange(128)[:, None], 0.0, -1e9).astype(np.float32)
    full = np.full((128, 128), -1e9, np.float32)
    zero = np.zeros((128, 128), np.float32)
    maskab = []
    for c in range(8):
        par = c % 2
        mA = tri if par == 0 else zero
        mB = full if par == 0 else tri
        maskab.append(np.ascontiguousarray(np.concatenate([mA, mB], axis=0)))
    maskabT = [np.ascontiguousarray(np.concatenate([m[0:128].T, m[128:256].T], axis=0))
               for m in maskab]

    wproj = _f32(np.asarray(L0_attn_w_project, np.float32).T)
    wout = _f32(np.asarray(L0_attn_w_out, np.float32).T)
    w1T = _f32(np.asarray(L0_ffn_w1, np.float32).T)
    w2T = _f32(np.asarray(L0_ffn_w2, np.float32).T)
    w3T = _f32(np.asarray(L0_ffn_w3, np.float32).T)
    wqkv1 = _f32(np.concatenate([np.asarray(L1_attn_W_q, np.float32),
                                 np.asarray(L1_attn_W_k, np.float32),
                                 np.asarray(L1_attn_W_v, np.float32)], axis=1))
    wo1 = _f32(np.asarray(L1_attn_W_o, np.float32))
    wbase = _f32(np.asarray(L1_ffn_W_base, np.float32))
    bbaseT = _f32(np.asarray(L1_ffn_b_base, np.float32).reshape(16, 128).T)
    a_s = _f32(np.asarray(L1_ffn_A_s, np.float32))
    b_s = _f32(np.asarray(L1_ffn_B_s, np.float32))
    a_r = _f32(np.asarray(L1_ffn_A_r, np.float32))
    b_r = _f32(np.asarray(L1_ffn_B_r, np.float32))
    wrout = _f32(np.asarray(L1_ffn_W_router, np.float32))
    brout = _f32(np.asarray(L1_ffn_b_router, np.float32).reshape(1, E))
    esel = np.zeros((E, E * 16), np.float32)
    for e in range(E):
        esel[e, e * 16:(e + 1) * 16] = 1.0

    nc = _get("fused", build_fused)
    in_maps = []
    for c in range(8):
        in_maps.append({
            "xT": xTs[c], "wproj": wproj, "wout": wout, "w1T": w1T, "w2T": w2T,
            "w3T": w3T, "wqkv1": wqkv1, "wo1": wo1, "wbase": wbase, "bbaseT": bbaseT,
            "a_s": a_s, "b_s": b_s, "a_r": a_r, "b_r": b_r, "wrout": wrout,
            "brout": brout, "identb": identb, "identf": identf, "esel": esel,
            "maskab": maskab[c], "maskabT": maskabT[c],
        })
    res = run_bass_kernel_spmd(nc, in_maps, core_ids=list(range(8))).results

    out = np.empty((B, 4096, D), np.float32)
    for c in range(8):
        seq, par = c // 2, c % 2
        rows = np.asarray(res[c]["outT"], np.float32).T
        out[seq].reshape(32, 128, D)[par::2] = rows.reshape(NT, 128, D)
    return out
